# version 05
# baseline (speedup 1.0000x reference)
import math
import numpy as np
from contextlib import ExitStack

import concourse.bass as bass
import concourse.mybir as mybir
from concourse.bass_utils import run_bass_kernel_spmd

F32 = mybir.dt.float32
BF16 = mybir.dt.bfloat16
I32 = mybir.dt.int32
U32 = mybir.dt.uint32
AF = mybir.ActivationFunctionType
ALU = mybir.AluOpType
AX = mybir.AxisListType
IOX = bass.IndirectOffsetOnAxis

T = 4096
D = 1024
NT = T // 128
ND = D // 8 // 16
NEG = -1.0e30

ENG2SEM = {"tensor": "pe", "vector": "dv", "scalar": "ac", "gpsimd": "gs", "sync": "ld"}


class Ctx:
    def __init__(self, nc, block, es):
        self.nc = nc
        self.block = block
        self.es = es
        self._cnt = {}
        self._sems = {}
        self.uid = 0

    def sem(self, name):
        if name not in self._sems:
            self._sems[name] = self.es.enter_context(self.nc.semaphore(name))
            self._cnt[name] = 0
        return self._sems[name]

    def bump(self, name, by=1):
        self.sem(name)
        self._cnt[name] += by
        return self._cnt[name]

    def cnt(self, name):
        self.sem(name)
        return self._cnt[name]

    def sb(self, name, shape, dtype):
        return self.es.enter_context(self.nc.sbuf_tensor(name, shape, dtype))

    def ps(self, name, shape, dtype=F32):
        return self.es.enter_context(self.nc.psum_tensor(name, shape, dtype))

    def fresh(self, prefix):
        self.uid += 1
        return f"{prefix}{self.uid}"

    def make_pool(self):
        self.pool = {
            "PA": self.ps("poolPA", [128, 1024], F32),
            "PB": self.ps("poolPB", [128, 1024], F32),
            "PC": self.ps("poolPC", [128, 512], F32),
            "PD": self.ps("poolPD", [128, 512], F32),
            "PR": self.ps("poolPR", [1, 1024], F32),
        }
        self.xln = [self.sb(f"arXL{c}", [128, T], BF16) for c in range(8)]
        self.scr6 = [self.sb(f"arS6{i}", [128, 512], F32) for i in range(6)]
        self.wtiles = [self.sb(f"arWT{r}", [128, 1024], BF16) for r in range(8)]
        self._ar = {
            "bf": self.sb("arBF", [128, 16384], BF16),
            "f32b": self.sb("arF32B", [128, 4096], F32),
            "bf2": self.sb("arBF2", [128, 4096], BF16),
            "f32": self.sb("arF32", [128, 3072], F32),
            "i32": self.sb("arI32", [128, 1024], I32),
        }
        self._ar_ptr = {k: 0 for k in self._ar}
        self.evbf = [self.sb(f"arEV{i}", [128, 1024], BF16) for i in range(2)]
        self.zrow = self.sb("arZR", [1, 512], F32)
        self.rowbf = [self.sb(f"arRB{i}", [1, 1024], BF16) for i in range(2)]
        self.rowbf2 = self.sb("arRB2", [1, 1024], F32)
        self.onescalar = self.sb("arONE1", [1, 1], F32)
        self._rowbf_i = 0
        self.colsf = [self.sb(f"arCF{i}", [128, 8], F32) for i in range(6)]
        self._colsf_i = 0

    def reset_arenas(self):
        self._ar_ptr = {k: 0 for k in self._ar}

    def carve(self, name, cols):
        p = self._ar_ptr[name]
        self._ar_ptr[name] = p + cols
        assert self._ar_ptr[name] <= self._ar[name].shape[1], \
            f"arena {name} overflow: {self._ar_ptr[name]}"
        return self._ar[name][:, p:p + cols]

    def step(self, engine, emit, waits=(), inc=1, inc_sem=None):
        sem_name = inc_sem or ENG2SEM[engine]
        sem = self.sem(sem_name)
        wl = [(self.sem(n), v) for (n, v) in waits]

        def body(e):
            for s, v in wl:
                e.wait_ge(s, v)
            h = emit(e)
            if inc:
                h.then_inc(sem, inc)
        getattr(self.block, engine)(body)
        if inc:
            return self.bump(sem_name, inc)
        return self.cnt(sem_name)

    def gate(self, waits):
        wl = [(self.sem(n), v) for (n, v) in waits]
        for eng in ("tensor", "vector", "scalar", "gpsimd", "sync"):
            def body(e, wl=wl):
                for s, v in wl:
                    e.wait_ge(s, v)
            getattr(self.block, eng)(body)

    def dma(self, engine, out, in_, waits=(), sem_name=None, inc=16, **kw):
        sem_name = sem_name or ("gs" if engine == "gpsimd" else "ld")
        sem = self.sem(sem_name)
        wl = [(self.sem(n), v) for (n, v) in waits]

        def body(e):
            for s, v in wl:
                e.wait_ge(s, v)
            if "in_offset" in kw or "out_offset" in kw:
                e.indirect_dma_start(out=out, out_offset=kw.get("out_offset"),
                                     in_=in_, in_offset=kw.get("in_offset"),
                                     bounds_check=kw.get("bounds_check"),
                                     oob_is_err=kw.get("oob_is_err", True),
                                     compute_op=kw.get("compute_op", ALU.bypass)
                                     ).then_inc(sem, 16)
            else:
                e.dma_start(out=out, in_=in_, **kw).then_inc(sem, 16)
        getattr(self.block, engine)(body)
        return self.bump(sem_name, inc)




def emit_layernorm(ctx, xln_sb, C, xT_dram=None, src_sb=None, eps=1e-5,
                   gamma=None, beta=None, rms=False, scr6=None, act=None,
                   out_f32_dram=None, ready=()):
    g = ctx.fresh("ln")
    if scr6 is None:
        scr6 = [ctx.sb(g + f"s6{i}", [128, 512], F32) for i in range(6)]
    if not hasattr(ctx, "_ln_shared"):
        ctx._ln_shared = {
            "rowb": [ctx.sb("lnSrow" + str(i), [1, 512], F32) for i in range(2)],
            "pk": ctx.sb("lnSpk", [128, 64], F32),
            "mu_bc": ctx.sb("lnSmub", [128, T], F32),
            "rs_bc": ctx._ar["f32b"][:, 0:T],
        }
    LS = ctx._ln_shared
    xb = scr6[0:2]
    sq = scr6[2:4]
    ps_s = ctx.pool["PR"][:, 0:512]
    ps_q = ctx.pool["PR"][:, 512:1024]
    rowb = LS["rowb"]
    pk = LS["pk"]
    mu_p, rs_p = pk[:, 0:32], pk[:, 32:64]
    ln_d = ctx.ln_dram
    bc_ps = [ctx.pool["PC"], ctx.pool["PD"]]
    mu_bc = LS["mu_bc"]
    rs_bc = LS["rs_bc"]
    apl = scr6[4:6]
    ones = C["ones"]

    def xsrc(i, tb, c, waits):
        if src_sb is not None:
            return src_sb[c][:, tb * 512:(tb + 1) * 512], list(waits)
        sem = f"lnx{i % 2}"
        lw = ctx.dma("sync", xb[i % 2][:],
                     xT_dram[c * 128:(c + 1) * 128, tb * 512:(tb + 1) * 512],
                     waits=waits, sem_name=sem)
        return xb[i % 2][:], [(sem, lw)]

    mm_done = {}
    ev_done = {}
    for tb in range(8):
        for c in range(8):
            i = 8 * tb + c
            w = list(ready) if i < 2 else []
            if i >= 2:
                w.append(("pe", mm_done[i - 2]))
            if c == 0 and tb > 0:
                w.append(("dv", ev_done[tb - 1]))
            xap, w2 = xsrc(i, tb, c, w)
            aw = ctx.step("scalar", lambda sc, xap=xap, b=sq[i % 2]: sc.activation(
                b[:], xap, AF.Square), waits=w2 + w)

            def mm(t, xap=xap, b=sq[i % 2], c=c):
                if not rms:
                    t.matmul(ps_s, ones[:, 0:1], xap, start=(c == 0), stop=(c == 7))
                return t.matmul(ps_q, ones[:, 0:1], b[:], start=(c == 0), stop=(c == 7))
            mm_done[i] = ctx.step("tensor", mm, waits=[("ac", aw)] + w)
        wst = ([("lnr0", ctx.cnt("lnr0")), ("lnr1", ctx.cnt("lnr1"))]
               if tb >= 1 else [])
        d = ctx.step("vector", lambda v, tb=tb: v.tensor_copy(rowb[0][:, :], ps_q),
                     waits=[("pe", mm_done[8 * tb + 7])] + wst)
        ctx.dma("sync", ln_d[1, tb * 512:(tb + 1) * 512].rearrange("(o f) -> o f", o=1),
                rowb[0][:, :], waits=[("dv", d)], sem_name="lnr0")
        if not rms:
            d = ctx.step("vector", lambda v, tb=tb: v.tensor_copy(rowb[1][:, :], ps_s),
                         waits=[("dv", d)])
            ctx.dma("sync", ln_d[0, tb * 512:(tb + 1) * 512].rearrange("(o f) -> o f", o=1),
                    rowb[1][:, :], waits=[("dv", d)], sem_name="lnr1")
        ev_done[tb] = d

    wrows = [("lnr0", ctx.cnt("lnr0")), ("lnr1", ctx.cnt("lnr1"))]
    lsum = ctx.dma("sync", pk[:, 32:64], ln_d[1, :].rearrange("(p c) -> p c", c=32),
                   waits=wrows, sem_name="lnp")
    if not rms:
        lsum = ctx.dma("sync", pk[:, 0:32], ln_d[0, :].rearrange("(p c) -> p c", c=32),
                       sem_name="lnp")
    d = ctx.step("vector", lambda v: v.tensor_scalar(rs_p, rs_p, 1.0 / D, None, op0=ALU.mult),
                 waits=[("lnp", lsum)])
    if not rms:
        d = ctx.step("vector", lambda v: v.tensor_scalar(mu_p, mu_p, 1.0 / D, None, op0=ALU.mult),
                     waits=[("dv", d)])
        d = ctx.step("vector", lambda v: v.scalar_tensor_tensor(
            sq[0][:, 0:32], mu_p, -1.0, mu_p, op0=ALU.mult, op1=ALU.mult), waits=[("dv", d)])
        d = ctx.step("vector", lambda v: v.tensor_tensor(rs_p, rs_p, sq[0][:, 0:32], ALU.add),
                     waits=[("dv", d)])
    d = ctx.step("vector", lambda v: v.tensor_scalar(rs_p, rs_p, eps, None, op0=ALU.add),
                 waits=[("dv", d)])
    a = ctx.step("scalar", lambda sc: sc.activation(rs_p, rs_p, AF.Sqrt), waits=[("dv", d)])
    d = ctx.step("vector", lambda v: v.reciprocal(rs_p, rs_p), waits=[("ac", a)])
    wmu = ctx.dma("sync", ln_d[1, :].rearrange("(p c) -> p c", c=32), rs_p,
                  waits=[("dv", d)], sem_name="lnq")
    if not rms:
        wmu = ctx.dma("sync", ln_d[0, :].rearrange("(p c) -> p c", c=32), mu_p,
                      waits=[("dv", d)], sem_name="lnq")

    srcs = [(1, rs_bc)] if rms else [(1, rs_bc), (0, mu_bc)]
    pdone = {}
    j = 0
    for ridx, dst in srcs:
        for tb in range(8):
            s = slice(tb * 512, (tb + 1) * 512)
            lw = [("lnq", wmu)] if j < 2 else [("pe", pdone[j - 2])]
            sem = f"lnr{j % 2}"
            lr = ctx.dma("sync", rowb[j % 2][:, :],
                         ln_d[ridx, s].rearrange("(o f) -> o f", o=1), waits=lw,
                         sem_name=sem)
            pdone[j] = ctx.step("tensor", lambda t, j=j: t.matmul(
                bc_ps[j % 2][:, :], ones[0:1, :], rowb[j % 2][:, :], start=True, stop=True),
                waits=[(sem, lr), ("dv", d)])
            d = ctx.step("vector", lambda v, s=s, j=j, dst=dst: v.tensor_copy(
                dst[:, s], bc_ps[j % 2][:, :]), waits=[("pe", pdone[j])])
            j += 1
    bc_done = d

    i = 0
    done = {}
    for c in range(8):
        for tb in range(8):
            s = slice(tb * 512, (tb + 1) * 512)
            w = [("dv", bc_done)] if i < 2 else [done[i - 2]]
            xap, w2 = xsrc(i, tb, c, w)
            ap = apl[i % 2]
            if rms:
                dd = ctx.step("vector", lambda v, xap=xap, s=s, ap=ap: v.tensor_tensor(
                    ap[:, :], xap, rs_bc[:, s], ALU.mult), waits=w2 + w)
            else:
                dd = ctx.step("vector", lambda v, xap=xap, s=s, ap=ap: v.tensor_tensor(
                    ap[:, :], xap, mu_bc[:, s], ALU.subtract), waits=w2 + w)
                dd = ctx.step("vector", lambda v, s=s, ap=ap: v.tensor_tensor(
                    ap[:, :], ap[:, :], rs_bc[:, s], ALU.mult), waits=[("dv", dd)])
            if gamma is not None and beta is not None:
                dd = ctx.step("vector", lambda v, s=s, c=c, ap=ap: v.tensor_scalar(
                    ap[:, :], ap[:, :], gamma[:, c:c + 1], beta[:, c:c + 1],
                    op0=ALU.mult, op1=ALU.add), waits=[("dv", dd)])
            elif gamma is not None:
                dd = ctx.step("vector", lambda v, s=s, c=c, ap=ap: v.tensor_scalar(
                    ap[:, :], ap[:, :], gamma[:, c:c + 1], None,
                    op0=ALU.mult), waits=[("dv", dd)])
            tail_w = [("dv", dd)]
            if out_f32_dram is not None:
                lnw = ctx.dma("sync", out_f32_dram[c * 128:(c + 1) * 128, s], ap[:, :],
                              waits=[("dv", dd)], sem_name="lnw")
                tail_w = [("lnw", lnw)]
            if act is not None:
                done[i] = ("ac", ctx.step("scalar", lambda sc, s=s, c=c, ap=ap: sc.activation(
                    xln_sb[c][:, s], ap[:, :], act), waits=tail_w))
            else:
                done[i] = ("dv", ctx.step("vector", lambda v, s=s, c=c, ap=ap: v.tensor_copy(
                    xln_sb[c][:, s], ap[:, :]), waits=tail_w))
            i += 1
    return [done[i - 1], ("dv", ctx.cnt("dv")), ("ac", ctx.cnt("ac"))]




def load_weight_bf16(ctx, w_dram, R, C_, out=None, waits=()):
    g = ctx.fresh("w")
    nr = R // 128
    if out is None:
        out = [ctx.sb(g + f"b{r}", [128, C_], BF16) for r in range(nr)]
    gw = None
    for r in range(nr):
        gw = ctx.dma("gpsimd", out[r][:, 0:C_], w_dram[r * 128:(r + 1) * 128, :],
                     waits=(waits if r == 0 else ()))
    return out, gw


def load_row_bf16(ctx, w_dram_row, C_):
    assert C_ <= 1024
    o = ctx.rowbf[ctx._rowbf_i % 2][:, 0:C_]
    ctx._rowbf_i += 1
    gw = ctx.dma("gpsimd", o, w_dram_row, waits=[("dv", ctx.cnt("dv")), ("pe", ctx.cnt("pe"))])
    return o, gw


def load_cols_f32(ctx, vec_dram, n):
    assert n <= 8
    o = ctx.colsf[ctx._colsf_i % 6][:, 0:n]
    ctx._colsf_i += 1
    row = ctx.rowbf2
    w = [("dv", ctx.cnt("dv")), ("pe", ctx.cnt("pe"))]
    nrow = max(1, (n * 128) // 512)
    for j in range(nrow):
        lw = ctx.dma("sync", row[:, j * 512:(j + 1) * 512],
                     vec_dram[j * 512:(j + 1) * 512].rearrange("(o f) -> o f", o=1),
                     sem_name="csr", waits=(w if j == 0 else ()))
    d = None
    for c in range(n):
        p = ctx.step("tensor", lambda t, c=c: t.matmul(
            ctx.pool["PR"][0:128, 0:1] if False else ctx.pool["PC"][:, 0:1],
            row[:, c * 128:(c + 1) * 128], ctx.onescalar[:, 0:1],
            start=True, stop=True), waits=[("csr", lw)] if c == 0 else [("dv", d)])
        d = ctx.step("vector", lambda v, c=c: v.tensor_copy(o[:, c:c + 1],
                     ctx.pool["PC"][:, 0:1]), waits=[("pe", p)])
    return o, d




def emit_proj_fm(ctx, xln_sb, Wt, out_dram, ready, bias_cols=None, out_sb=None,
                 nk=8, nm=8, scale=None):
    g = ctx.fresh("pf")
    ps = [ctx.pool["PC"], ctx.pool["PD"]]
    ev = [ctx.evbf[0][:, 0:512], ctx.evbf[1][:, 0:512]]
    dd_prev = {}
    st_prev = {}
    i = 0
    last = None
    for m in range(nm):
        for nb in range(T // 512):
            s = slice(nb * 512, nb * 512 + 512)
            w = list(ready) if i < 2 else [("dv", dd_prev[i - 2])]

            def mm(t, m=m, s=s, i=i):
                for k in range(nk):
                    h = t.matmul(ps[i % 2][:, :], Wt[k][:, m * 128:(m + 1) * 128],
                                 xln_sb[k][:, s], start=(k == 0), stop=(k == nk - 1))
                return h
            p = ctx.step("tensor", mm, waits=w)
            ssem = f"pfs{i % 2}"
            wst = [("pe", p), (ssem, ctx.cnt(ssem))]
            if bias_cols is not None:
                dd = ctx.step("vector", lambda v, m=m, i=i: v.tensor_scalar(
                    ev[i % 2][:, :], ps[i % 2][:, :], bias_cols[:, m:m + 1],
                    None, op0=ALU.add), waits=wst)
                if scale is not None:
                    dd = ctx.step("vector", lambda v, i=i: v.tensor_scalar(
                        ev[i % 2][:, :], ev[i % 2][:, :], scale, None, op0=ALU.mult),
                        waits=[("dv", dd)])
            elif scale is not None:
                dd = ctx.step("vector", lambda v, m=m, i=i: v.tensor_scalar(
                    ev[i % 2][:, :], ps[i % 2][:, :], scale, None, op0=ALU.mult),
                    waits=wst)
            else:
                dd = ctx.step("vector", lambda v, m=m, i=i: v.tensor_copy(
                    ev[i % 2][:, :], ps[i % 2][:, :]), waits=wst)
            if out_sb is not None:
                dd = ctx.step("vector", lambda v, m=m, s=s, i=i: v.tensor_copy(
                    out_sb[m][:, s], ev[i % 2][:, :]), waits=[("dv", dd)])
            if out_dram is not None:
                last = ctx.dma("sync", out_dram[m * 128:(m + 1) * 128, s],
                               ev[i % 2][:, :], waits=[("dv", dd)], sem_name=ssem)
            dd_prev[i] = dd
            i += 1
    return dd_prev[i - 1], last


def emit_proj_tm(ctx, xln_sb, Wm, out_dram, ready, C, bias_row=None, nk=8,
                 nout=1024, out_sb=None):
    g = ctx.fresh("pt")
    assert nout <= 1024
    ps = [ctx.pool["PA"][:, 0:nout], ctx.pool["PB"][:, 0:nout]]
    ev = [ctx.evbf[0][:, 0:nout], ctx.evbf[1][:, 0:nout]]
    dd_prev = {}
    last = None
    for i in range(NT):
        w = list(ready) if i < 2 else [("dv", dd_prev[i - 2])]

        def mm(t, i=i):
            for half in range(max(1, nout // 512)):
                hs = slice(half * 512, min(nout, half * 512 + 512))
                for k in range(nk):
                    h = t.matmul(ps[i % 2][:, hs], xln_sb[k][:, i * 128:(i + 1) * 128],
                                 Wm[k][:, hs], start=(k == 0),
                                 stop=(k == nk - 1 and bias_row is None))
                if bias_row is not None:
                    h = t.matmul(ps[i % 2][:, hs], C["ones_bf"][0:1, :], bias_row[:, hs],
                                 start=False, stop=True)
            return h
        p = ctx.step("tensor", mm, waits=w)
        ssem = f"pfs{i % 2}"
        dd = ctx.step("vector", lambda v, i=i: v.tensor_copy(ev[i % 2][:, :], ps[i % 2][:, :]),
                      waits=[("pe", p), (ssem, ctx.cnt(ssem))])
        if out_sb is not None:
            dd = ctx.step("vector", lambda v, i=i: v.tensor_copy(
                out_sb[i][:, :], ev[i % 2][:, :]), waits=[("dv", dd)])
        if out_dram is not None:
            last = ctx.dma("sync", out_dram[i * 128:(i + 1) * 128, :], ev[i % 2][:, :],
                           waits=[("dv", dd)], sem_name=ssem)
        dd_prev[i] = dd
    return dd_prev[NT - 1], last




def emit_l1_attn(ctx, xT_in, xT_out, wd, C, scr):
    g = ctx.fresh("l1a")
    scr6 = ctx.scr6
    xln = ctx.xln
    ln_w = emit_layernorm(ctx, xln, C, xT_dram=xT_in, scr6=scr6)

    wtiles = ctx.wtiles
    b1c, _ = load_cols_f32(ctx, wd["b1"], 8)
    b2c, _ = load_cols_f32(ctx, wd["b2"], 8)
    b4c, _ = load_cols_f32(ctx, wd["b4"], 8)
    b3r, d_b3 = load_row_bf16(ctx, wd["b3"].rearrange("(o d) -> o d", o=1), D)

    _, gW = load_weight_bf16(ctx, wd["W1"], D, D, out=wtiles)
    dq, _ = emit_proj_fm(ctx, xln, wtiles, scr["qT"],
                         ln_w + [("gs", gW), ("dv", ctx.cnt("dv"))],
                         bias_cols=b1c, scale=1.0 / 32.0)
    _, gW = load_weight_bf16(ctx, wd["W2"], D, D, out=wtiles, waits=[("dv", dq)])
    dk, _ = emit_proj_fm(ctx, xln, wtiles, scr["kT"], [("dv", dq), ("gs", gW)], bias_cols=b2c)
    _, gW = load_weight_bf16(ctx, wd["W3"], D, D, out=wtiles, waits=[("dv", dk)])
    dv_, stv = emit_proj_tm(ctx, xln, wtiles, scr["V"], [("dv", dk), ("gs", gW)], C, bias_row=b3r)
    _, gW4 = load_weight_bf16(ctx, wd["W4"], D, D, out=wtiles, waits=[("dv", dv_)])
    W4 = wtiles
    proj_done = [("dv", dv_), ("gs", gW4), ("pfs0", ctx.cnt("pfs0")),
                 ("pfs1", ctx.cnt("pfs1"))]

    GW = 256
    qTg = ctx.carve("bf", 8 * GW)
    kTi = [ctx.carve("bf2", 1024) for i in range(2)]
    Vt = [ctx.carve("bf2", 1024) for i in range(2)]
    pTg = ctx.carve("bf", NT * GW)
    zrow = ctx.zrow
    rz_bc = ctx.carve("f32", 512)
    aT = ctx.carve("bf", 8 * GW)
    yscr = scr6[0:2]
    xres = scr6[2:4]
    psS = [ctx.pool["PC"], ctx.pool["PD"]]
    psZ = ctx.pool["PR"][:, 0:512]
    psPV = [ctx.pool["PA"], ctx.pool["PB"]]

    qT_v = scr["qT"].rearrange("(c p) t -> p c t", p=128)
    kT_v = scr["kT"].rearrange("(c p) t -> p c t", p=128)

    prev_tail = ctx.cnt("dv")
    NG = T // GW
    for gi in range(NG):
        ts = slice(gi * GW, gi * GW + GW)
        ni = NT
        lq = ctx.dma("sync", qTg[:, :], qT_v[:, :, ts], waits=proj_done + [("dv", prev_tail)],
                     sem_name="qg")
        exp_done = {}
        z_done = None
        for i in range(ni):
            w = [("qg", lq)] if i < 2 else [("ac", exp_done[i - 2])]
            lk = ctx.dma("sync", kTi[i % 2][:, :],
                         kT_v[:, :, i * 128:(i + 1) * 128], waits=w, sem_name=f"kt{i % 2}")

            def mm(t, i=i, ts=ts):
                for k in range(8):
                    h = t.matmul(psS[i % 2][:, 0:GW], kTi[i % 2][:, k * 128:(k + 1) * 128],
                                 qTg[:, k * GW:(k + 1) * GW], start=(k == 0), stop=(k == 7))
                return h
            p = ctx.step("tensor", mm, waits=[(f"kt{i % 2}", lk)] + ([("ac", exp_done[i - 2])] if i >= 2 else [("dv", prev_tail)]))
            ex = ctx.step("scalar", lambda sc, i=i: sc.activation(
                pTg[:, i * GW:(i + 1) * GW], psS[i % 2][:, 0:GW], AF.Exp),
                waits=[("pe", p)])
            zw = [("ac", ex)]
            exp_done[i] = ex
            z_done = ctx.step("tensor", lambda t, i=i, ni=ni: t.matmul(
                psZ[:, 0:GW], C["onesb"][:, 0:1], pTg[:, i * GW:(i + 1) * GW],
                start=(i == 0), stop=(i == ni - 1)), waits=zw)
        d1 = ctx.step("vector", lambda v: v.reciprocal(zrow[:, 0:GW], psZ[:, 0:GW]), waits=[("pe", z_done)])
        p1 = ctx.step("tensor", lambda t: t.matmul(psS[0][:, 0:GW], C["ones"][0:1, :],
                      zrow[:, 0:GW], start=True, stop=True), waits=[("dv", d1)])
        d2 = ctx.step("vector", lambda v: v.tensor_copy(rz_bc[:, 0:GW], psS[0][:, 0:GW]),
                      waits=[("pe", p1)])
        ev_half = {}
        pv_done = None
        for h in range(2):
            for i in range(ni):
                if i < 2:
                    wv = [("dv", ctx.cnt("dv"))] if pv_done is None else [
                        ("pe", pv_done), ("dv", ctx.cnt("dv"))]
                else:
                    wv = [("pe", pv_done)]
                lv = ctx.dma("sync", Vt[i % 2][:, :],
                             scr["V"][i * 128:(i + 1) * 128, :], waits=wv,
                             sem_name=f"vt{i % 2}")

                def mmv(t, i=i, h=h, ni=ni):
                    for m in range(4):
                        hh = t.matmul(psPV[m // 2][:, (m % 2) * 512:(m % 2) * 512 + GW],
                                      Vt[i % 2][:, (4 * h + m) * 128:(4 * h + m + 1) * 128],
                                      pTg[:, i * GW:(i + 1) * GW],
                                      start=(i == 0), stop=(i == ni - 1))
                    return hh
                pv_done = ctx.step("tensor", mmv, waits=[(f"vt{i % 2}", lv)])
            for m in range(4):
                ev_half[4 * h + m] = ctx.step("vector", lambda v, h=h, m=m: v.tensor_tensor(
                    aT[:, (4 * h + m) * GW:(4 * h + m + 1) * GW],
                    psPV[m // 2][:, (m % 2) * 512:(m % 2) * 512 + GW],
                    rz_bc[:, 0:GW], ALU.mult), waits=[("pe", pv_done)])
        tail = None
        for mp in range(8):
            lr = ctx.dma("sync", xres[mp % 2][:, 0:GW],
                         xT_in[mp * 128:(mp + 1) * 128, ts],
                         waits=[("dv", ev_half[7])] if mp < 2 else [("dv", ctx.cnt("dv"))],
                         sem_name=f"xr{mp % 2}")

            def mmy(t, mp=mp):
                for k in range(8):
                    h = t.matmul(psS[mp % 2][:, 0:GW], W4[k][:, mp * 128:(mp + 1) * 128],
                                 aT[:, k * GW:(k + 1) * GW], start=(k == 0), stop=(k == 7))
                return h
            p = ctx.step("tensor", mmy, waits=[("dv", ev_half[7])] if mp < 2 else [("dv", ctx.cnt("dv"))])
            ysb = yscr[mp % 2]
            dd = ctx.step("vector", lambda v, mp=mp, ysb=ysb: v.tensor_scalar(
                ysb[:, 0:GW], psS[mp % 2][:, 0:GW], b4c[:, mp:mp + 1], None, op0=ALU.add),
                waits=[("pe", p), (f"xr{mp % 2}", lr),
                       (f"sy{mp % 2}", ctx.cnt(f"sy{mp % 2}"))])
            dd = ctx.step("vector", lambda v, mp=mp, ysb=ysb: v.tensor_tensor(
                ysb[:, 0:GW], ysb[:, 0:GW], xres[mp % 2][:, 0:GW], ALU.add), waits=[("dv", dd)])
            tail = ctx.dma("sync", xT_out[mp * 128:(mp + 1) * 128, ts], ysb[:, 0:GW],
                           waits=[("dv", dd)], sem_name=f"sy{mp % 2}")
            prev_tail = dd
    return ctx.cnt("st")




def emit_consts(ctx, cd):
    C = {}
    C["ones"] = ctx.sb("c_ones", [128, 128], F32)
    C["ident"] = ctx.sb("c_ident", [128, 128], F32)
    C["neg200"] = ctx.sb("c_n200", [128, 1], F32)
    C["identb"] = ctx.sb("c_identb", [128, 128], BF16)
    C["iota16"] = ctx.sb("c_iota16", [128, 16], F32)
    C["ones_bf"] = ctx.sb("c_onesbf", [1, 128], BF16)
    C["onesb"] = ctx.sb("c_onesb", [128, 128], BF16)
    gi = ctx.step("gpsimd", lambda gp: gp.iota(C["ident"][:, :], pattern=[[1, 128]],
                  base=0, channel_multiplier=-1, allow_small_or_imprecise_dtypes=True),
                  inc_sem="gsi", inc=1)
    di = ctx.step("vector", lambda v: v.tensor_scalar(C["ident"][:, :], C["ident"][:, :],
                  0.0, None, op0=ALU.is_equal), waits=[("gsi", gi)])
    di = ctx.step("vector", lambda v: v.tensor_copy(C["identb"][:, :], C["ident"][:, :]),
                  waits=[("dv", di)])
    gi2 = ctx.step("gpsimd", lambda gp: gp.iota(C["iota16"][:, :], pattern=[[1, 16]],
                   base=0, channel_multiplier=0, allow_small_or_imprecise_dtypes=True),
                   waits=[("dv", di)], inc_sem="gsi", inc=1)
    di = ctx.step("vector", lambda v: v.tensor_copy(C["iota16"][:, :], C["iota16"][:, :]),
                  waits=[("gsi", gi2)])
    d1 = ctx.step("vector", lambda v: v.memset(ctx.onescalar[:, :], 1.0), waits=[("dv", di)])
    d1 = ctx.step("vector", lambda v: v.memset(C["neg200"][:, :], -200.0), waits=[("dv", d1)])
    d1 = ctx.step("vector", lambda v: v.memset(C["ones"][:, :], 1.0), waits=[("dv", d1)])
    d2 = ctx.step("vector", lambda v: v.memset(C["ones_bf"][:, :], 1.0), waits=[("dv", d1)])
    d2 = ctx.step("vector", lambda v: v.memset(C["onesb"][:, :], 1.0), waits=[("dv", d2)])
    C["ready"] = [("dv", d2)]
    return C


def host_consts():
    p = np.arange(128).reshape(128, 1)
    f = np.arange(512).reshape(1, 512)
    import ml_dtypes
    diag = np.zeros((4, 128, 512), dtype=ml_dtypes.bfloat16)
    for sh in range(4):
        diag[sh] = (f - p - 128 * sh >= 0).astype(ml_dtypes.bfloat16)
    return {"c_diag_d": diag}


def decl_consts(nc):
    return {"diag": nc.dram_tensor("c_diag_d", [4, 128, 512], BF16, kind="ExternalInput").ap()}



L1A_WNAMES = ["W1", "W2", "W3", "W4", "b1", "b2", "b3", "b4"]


def decl_l1_attn(nc):
    wd = {}
    for k in ("W1", "W2", "W3", "W4"):
        wd[k] = nc.dram_tensor(f"l1a_{k}", [D, D], F32, kind="ExternalInput").ap()
    for k in ("b1", "b2", "b3", "b4"):
        wd[k] = nc.dram_tensor(f"l1a_{k}", [D], F32, kind="ExternalInput").ap()
    scr = {
        "qT": nc.dram_tensor("l1a_qT", [D, T], BF16, kind="Internal").ap(),
        "kT": nc.dram_tensor("l1a_kT", [D, T], BF16, kind="Internal").ap(),
        "V": nc.dram_tensor("l1a_V", [T, D], BF16, kind="Internal").ap(),
    }
    return wd, scr


def stage_l1_attn(inp):
    d = {}
    for k in ("W1", "W2", "W3", "W4", "b1", "b2", "b3", "b4"):
        d[f"l1a_{k}"] = np.ascontiguousarray(inp[f"L1_attn_{k}"].astype(np.float32))
    return d




def build_program(block_list):
    nc = bass.Bass("TRN2", target_bir_lowering=False)
    xT_in = nc.dram_tensor("xT_in", [D, T], F32, kind="ExternalInput").ap()
    xT_out = nc.dram_tensor("xT_out", [D, T], F32, kind="ExternalOutput").ap()
    xT_mid = [nc.dram_tensor(f"xT_mid{i}", [D, T], F32, kind="Internal").ap()
              for i in range(2)]
    ln_dram = nc.dram_tensor("ln_scr", [2, T], F32, kind="Internal").ap()
    cd = decl_consts(nc)
    decls = {}
    for b in set(block_list):
        decls[b] = DECLS[b](nc)

    es = ExitStack()
    with es:
        block = es.enter_context(nc.Block())
        ctx = Ctx(nc, block, es)
        ctx.ln_dram = ln_dram
        ctx.make_pool()
        C = emit_consts(ctx, cd)
        ctx.gate(C["ready"])
        cur = xT_in
        for i, b in enumerate(block_list):
            out = xT_out if i == len(block_list) - 1 else xT_mid[i % 2]
            wd, scr = decls[b]
            if i > 0:
                ctx.gate([(k, v) for k, v in ctx._cnt.items() if v > 0])
                ctx.reset_arenas()
            with nc.named_scope(b):
                EMITTERS[b](ctx, cur, out, wd, C, scr)
            cur = out
        snap = dict(ctx._cnt)

        def fin(e):
            for name, v in snap.items():
                if v > 0:
                    e.wait_ge(ctx.sem(name), v)
        block.sync(fin)
    return nc


EMITTERS = {"l1_attn": emit_l1_attn}
DECLS = {"l1_attn": decl_l1_attn}
STAGERS = {"l1_attn": stage_l1_attn}


def run_blocks(block_lists, x, inp, n_cores=8):
    xT = [np.ascontiguousarray(x[c].T.astype(np.float32)) for c in range(n_cores)]
    consts = host_consts()
    for bl in block_lists:
        nc = build_program(bl)
        stage = dict(consts)
        for b in set(bl):
            stage.update(STAGERS[b](inp))
        in_maps = []
        for c in range(n_cores):
            m = dict(stage)
            m["xT_in"] = xT[c]
            in_maps.append(m)
        res = run_bass_kernel_spmd(nc, in_maps, core_ids=list(range(n_cores)))
        xT = [np.asarray(res.results[c]["xT_out"]) for c in range(n_cores)]
    return np.stack([t.T for t in xT], axis=0)




def emit_l0_ffn(ctx, xT_in, xT_out, wd, C, scr):
    g = ctx.fresh("l0f")
    scr6 = ctx.scr6
    xln = ctx.xln
    ln_w = emit_layernorm(ctx, xln, C, xT_dram=xT_in, scr6=scr6)

    hfm = [ctx.carve("bf", 256) for f in range(32)]
    wfc = [ctx.carve("bf", 1024) for i in range(2)]
    wpm = ctx.carve("bf", 32 * 128)
    hscr = scr6[0]
    ybuf = hscr
    mem = scr6[1]
    spk = scr6[2]
    acc = scr6[3]
    xres0 = scr6[4]
    xres = [xres0, xres0]

    wfcT_v = wd["wfcT"].rearrange("(c p) f -> p c f", p=128)
    wpT_v = wd["wpT"].rearrange("(c p) m -> p c m", p=128)

    psH = [ctx.pool["PA"], ctx.pool["PB"]]
    psY = ctx.pool["PA"]

    for tq in range(16):
        ts = slice(tq * 256, tq * 256 + 256)
        hdone = {}
        for fc in range(32):
            if fc < 2:
                w = (list(ln_w) if tq == 0 else [("dv", last_dv)])
            else:
                w = [hdone[fc - 2]]
            lw = ctx.dma("gpsimd", wfc[fc % 2][:, :], wfcT_v[:, :, fc * 128:(fc + 1) * 128],
                         waits=w, sem_name=f"wf{fc % 2}")

            def mm(t, fc=fc, ts=ts):
                for k in range(8):
                    h = t.matmul(psH[fc % 2][:, 0:256], wfc[fc % 2][:, k * 128:(k + 1) * 128],
                                 xln[k][:, ts], start=(k == 0), stop=(k == 7))
                return h
            p = ctx.step("tensor", mm, waits=[(f"wf{fc % 2}", lw)] + w)
            dd = ctx.step("vector", lambda v, fc=fc: v.tensor_scalar(
                hscr[:, 0:256], psH[fc % 2][:, 0:256], 0.0, None, op0=ALU.max),
                waits=[("pe", p)] + ([hdone[fc - 1]] if fc >= 1 else []))
            hdone[fc] = ("ac", ctx.step("scalar", lambda sc, fc=fc: sc.activation(
                hfm[fc][:, :], hscr[:, 0:256], AF.Square), waits=[("dv", dd)]))
        for m in range(8):
            wm = [hdone[31]] if m == 0 else [("dv", last_dv)]
            lw = ctx.dma("gpsimd", wpm[:, :], wpT_v[:, :, m * 128:(m + 1) * 128],
                         waits=wm, sem_name="wp")
            lr = ctx.dma("sync", xres[m % 2][:, 0:256], xT_in[m * 128:(m + 1) * 128, ts],
                         waits=wm + [("sy0", ctx.cnt("sy0"))], sem_name="xr0")

            def mmy(t, m=m):
                for f in range(32):
                    h = t.matmul(psY[:, 0:256], wpm[:, f * 128:(f + 1) * 128],
                                 hfm[f][:, :], start=(f == 0), stop=(f == 31))
                return h
            p = ctx.step("tensor", mmy, waits=[("wp", lw), hdone[31]])
            d = ctx.step("vector", lambda v: v.tensor_copy(ybuf[:, 0:256], psY[:, 0:256]),
                         waits=[("pe", p)])
            W_ = slice(0, 256)
            d = ctx.step("vector", lambda v: v.tensor_copy(mem[:, W_], ybuf[:, W_]), waits=[("dv", d)])
            d = ctx.step("vector", lambda v: v.tensor_scalar(spk[:, W_], mem[:, W_], 1.0, None, op0=ALU.is_gt),
                         waits=[("dv", d)])
            d = ctx.step("vector", lambda v: v.tensor_copy(acc[:, W_], spk[:, W_]), waits=[("dv", d)])
            for s_ in range(4):
                d = ctx.step("vector", lambda v: v.scalar_tensor_tensor(
                    mem[:, W_], mem[:, W_], 0.9, ybuf[:, W_], op0=ALU.mult, op1=ALU.add), waits=[("dv", d)])
                d = ctx.step("vector", lambda v: v.tensor_tensor(
                    mem[:, W_], mem[:, W_], spk[:, W_], ALU.subtract), waits=[("dv", d)])
                d = ctx.step("vector", lambda v: v.tensor_scalar(
                    spk[:, W_], mem[:, W_], 1.0, None, op0=ALU.is_gt), waits=[("dv", d)])
                d = ctx.step("vector", lambda v: v.tensor_tensor(
                    acc[:, W_], acc[:, W_], spk[:, W_], ALU.add), waits=[("dv", d)])
            d = ctx.step("vector", lambda v, m=m: v.scalar_tensor_tensor(
                xres[m % 2][:, W_], acc[:, W_], 0.2, xres[m % 2][:, W_], op0=ALU.mult, op1=ALU.add),
                waits=[("dv", d), ("xr0", lr)])
            ctx.dma("sync", xT_out[m * 128:(m + 1) * 128, ts], xres[m % 2][:, W_],
                    waits=[("dv", d)], sem_name="sy0")
            last_dv = d
    return ctx.cnt("sy0")


def decl_l0_ffn(nc):
    wd = {
        "wfcT": nc.dram_tensor("l0f_wfcT", [D, 4096], F32, kind="ExternalInput").ap(),
        "wpT": nc.dram_tensor("l0f_wpT", [4096, D], F32, kind="ExternalInput").ap(),
    }
    return wd, {}


def stage_l0_ffn(inp):
    return {
        "l0f_wfcT": np.ascontiguousarray(inp["L0_ffn_W_fc"].astype(np.float32).T),
        "l0f_wpT": np.ascontiguousarray(inp["L0_ffn_W_proj"].astype(np.float32).T),
    }


EMITTERS["l0_ffn"] = emit_l0_ffn
DECLS["l0_ffn"] = decl_l0_ffn
STAGERS["l0_ffn"] = stage_l0_ffn




def emit_qlora(ctx, in_tiles, wd, idx, C, wtiles, bufs, out_dram=None,
               res_dram=None, res2_dram=None, out_final=None, ready=()):
    g = ctx.fresh("ql")
    qbuf, scbuf, qf, la_sb, lb_sb, tmpT, evb, resb = bufs
    qT_v = wd["qT"][idx]
    scT_v = wd["scT"][idx]
    dq_done = {}
    for j in range(16):
        c, hh = j // 2, j % 2
        os = slice(hh * 512, hh * 512 + 512)
        w = list(ready) if j < 2 else [("dv", dq_done[j - 2])]
        lq = ctx.dma("sync", qbuf[j % 2][:, :], qT_v[c * 128:(c + 1) * 128, os],
                     waits=w, sem_name=f"qq{j % 2}")
        ls = ctx.dma("sync", scbuf[j % 2][:, :], scT_v[c * 128:(c + 1) * 128, os],
                     waits=w, sem_name=f"qs{j % 2}")
        d = ctx.step("vector", lambda v, j=j: v.tensor_copy(qf[:, :], qbuf[j % 2][:, :]),
                     waits=[(f"qq{j % 2}", lq)] + w + ([("dv", dq_done[j - 1])] if j >= 1 else []))
        d = ctx.step("vector", lambda v: v.tensor_scalar(qf[:, :], qf[:, :], 1.0 / 7.5, -1.0,
                     op0=ALU.mult, op1=ALU.add), waits=[("dv", d)])
        dq_done[j] = ctx.step("vector", lambda v, c=c, os=os, j=j: v.tensor_tensor(
            wtiles[c][:, os], qf[:, :], scbuf[j % 2][:, :], ALU.mult),
            waits=[("dv", d), (f"qs{j % 2}", ls)])
    la_l = ctx.dma("gpsimd", la_sb.rearrange("p (c r) -> p c r", c=8),
                   wd["laT"][idx].rearrange("(c p) r -> p c r", p=128),
                   waits=ready, sem_name="gla")
    lb_l = ctx.dma("gpsimd", lb_sb[:, :], wd["lbT"][idx], waits=ready, sem_name="gla")
    bcols, bl = load_cols_f32(ctx, wd["b"][idx], 8)
    psT = ctx.pool["PC"]
    tdone = {}
    for nb in range(8):
        s = slice(nb * 512, nb * 512 + 512)
        w = ([("gla", lb_l)] + list(ready)) if nb == 0 else [("dv", tdone[nb - 1])]

        def mmt(t, s=s):
            for k in range(8):
                h = t.matmul(psT[0:32, :], la_sb[:, k * 32:(k + 1) * 32],
                             in_tiles[k][:, s], start=(k == 0), stop=(k == 7))
            return h
        p = ctx.step("tensor", mmt, waits=w)
        tdone[nb] = ctx.step("vector", lambda v, s=s: v.tensor_copy(tmpT[:, s], psT[0:32, :]),
                             waits=[("pe", p)])
    psM = [ctx.pool["PA"], ctx.pool["PB"]]
    i = 0
    dd_prev = {}
    last_st = None
    for m in range(8):
        for nb in range(8):
            s = slice(nb * 512, nb * 512 + 512)
            w = [("dv", max(tdone[7], dq_done[15]))] if i < 2 else [("dv", dd_prev[i - 2])]

            def mm(t, m=m, s=s, i=i):
                for k in range(8):
                    t.matmul(psM[i % 2][:, 0:512], wtiles[k][:, m * 128:(m + 1) * 128],
                             in_tiles[k][:, s], start=(k == 0), stop=False)
                return t.matmul(psM[i % 2][:, 0:512], lb_sb[:, m * 128:(m + 1) * 128],
                                tmpT[:, s], start=False, stop=True)
            p = ctx.step("tensor", mm, waits=w)
            ssem = f"qo{i % 2}"
            dd = ctx.step("vector", lambda v, m=m, i=i: v.tensor_scalar(
                evb[i % 2][:, :], psM[i % 2][:, 0:512], bcols[:, m:m + 1], None, op0=ALU.add),
                waits=[("pe", p), ("dv", bl), (ssem, ctx.cnt(ssem))])
            if res_dram is not None:
                lr = ctx.dma("sync", resb[i % 2][:, :], res_dram[m * 128:(m + 1) * 128, s],
                             waits=w, sem_name=f"qr{i % 2}")
                dd = ctx.step("vector", lambda v, i=i: v.tensor_tensor(
                    evb[i % 2][:, :], evb[i % 2][:, :], resb[i % 2][:, :], ALU.add),
                    waits=[("dv", dd), (f"qr{i % 2}", lr)])
            if res2_dram is not None:
                lr2 = ctx.dma("sync", resb[i % 2][:, :], res2_dram[m * 128:(m + 1) * 128, s],
                              waits=[("dv", dd)], sem_name=f"qr{i % 2}")
                dd = ctx.step("vector", lambda v, i=i: v.tensor_tensor(
                    evb[i % 2][:, :], evb[i % 2][:, :], resb[i % 2][:, :], ALU.add),
                    waits=[("dv", dd), (f"qr{i % 2}", lr2)])
            last_st = ctx.dma("sync", out_dram[m * 128:(m + 1) * 128, s], evb[i % 2][:, :],
                              waits=[("dv", dd)], sem_name=ssem)
            dd_prev[i] = dd
            i += 1
    return [("dv", dd_prev[i - 1]), ("qo0", ctx.cnt("qo0")), ("qo1", ctx.cnt("qo1"))]


def emit_l1_ffn(ctx, xT_in, xT_out, wd, C, scr):
    g = ctx.fresh("l1f")
    scr6 = ctx.scr6
    xln = ctx.xln
    wtiles = ctx.wtiles
    bufs = (
        [ctx.carve("i32", 512) for i in range(2)],
        [ctx.carve("f32", 512) for i in range(2)],
        ctx.carve("f32", 512),
        ctx.carve("bf", 8 * 32),
        ctx.carve("bf", 1024)[0:32, :],
        ctx.carve("bf", T)[0:32, :],
        [scr6[0], scr6[1]],
        [scr6[2], scr6[3]],
    )
    ln_w = emit_layernorm(ctx, xln, C, xT_dram=xT_in, scr6=scr6,
                          out_f32_dram=scr["h0"])
    ready = ln_w + [("lnw", ctx.cnt("lnw"))]
    hcur = scr["h0"]
    for blk in range(3):
        i0, i1 = 2 * blk, 2 * blk + 1
        r = emit_qlora(ctx, xln, wd, i0, C, wtiles, bufs, out_dram=scr["y"],
                       ready=ready)
        gcols, _ = load_cols_f32(ctx, wd["lng"][blk], 8)
        bcols2, bl2 = load_cols_f32(ctx, wd["lnb"][blk], 8)
        ln2_w = emit_layernorm(ctx, xln, C, xT_dram=scr["y"], scr6=scr6,
                               gamma=gcols, beta=bcols2, act=AF.Gelu,
                               ready=r + [("dv", bl2)])
        last = blk == 2
        r = emit_qlora(ctx, xln, wd, i1, C, wtiles, bufs,
                       out_dram=(xT_out if last else scr["h1" if hcur is scr["h0"] else "h0"]),
                       res_dram=hcur, res2_dram=(xT_in if last else None),
                       ready=ln2_w)
        if not last:
            hcur = scr["h1" if hcur is scr["h0"] else "h0"]
        if not last:
            cast_done = None
            for c in range(8):
                lw = ctx.dma("gpsimd", xln[c][:, :], hcur[c * 128:(c + 1) * 128, :],
                             waits=(r if c == 0 else ()), sem_name="gla")
                cast_done = lw
            ready = [("gla", cast_done)]
    return ctx.cnt("qo0")


def decl_l1_ffn(nc):
    wd = {
        "qT": nc.dram_tensor("l1f_qT", [6, D, D], I32, kind="ExternalInput").ap(),
        "scT": nc.dram_tensor("l1f_scT", [6, D, D], F32, kind="ExternalInput").ap(),
        "laT": nc.dram_tensor("l1f_laT", [6, D, 32], F32, kind="ExternalInput").ap(),
        "lbT": nc.dram_tensor("l1f_lbT", [6, 32, D], F32, kind="ExternalInput").ap(),
        "b": nc.dram_tensor("l1f_b", [6, D], F32, kind="ExternalInput").ap(),
        "lng": nc.dram_tensor("l1f_lng", [3, D], F32, kind="ExternalInput").ap(),
        "lnb": nc.dram_tensor("l1f_lnb", [3, D], F32, kind="ExternalInput").ap(),
    }
    scr = {
        "h0": nc.dram_tensor("l1f_h0", [D, T], F32, kind="Internal").ap(),
        "h1": nc.dram_tensor("l1f_h1", [D, T], F32, kind="Internal").ap(),
        "y": nc.dram_tensor("l1f_y", [D, T], F32, kind="Internal").ap(),
    }
    return wd, scr


def stage_l1_ffn(inp):
    f = np.ascontiguousarray
    q = inp["L1_ffn_qweights"]
    sc = inp["L1_ffn_scales"]
    sc_rep = np.repeat(sc, 16, axis=2)
    return {
        "l1f_qT": f(q.transpose(0, 2, 1).astype(np.int32)),
        "l1f_scT": f(sc_rep.transpose(0, 2, 1).astype(np.float32)),
        "l1f_laT": f(inp["L1_ffn_lora_a"].transpose(0, 2, 1).astype(np.float32)),
        "l1f_lbT": f(inp["L1_ffn_lora_b"].transpose(0, 2, 1).astype(np.float32)),
        "l1f_b": f(inp["L1_ffn_biases"].astype(np.float32)),
        "l1f_lng": f(inp["L1_ffn_ln_gamma"].astype(np.float32)),
        "l1f_lnb": f(inp["L1_ffn_ln_beta"].astype(np.float32)),
    }


EMITTERS["l1_ffn"] = emit_l1_ffn
DECLS["l1_ffn"] = decl_l1_ffn
STAGERS["l1_ffn"] = stage_l1_ffn




def emit_out_proj(ctx, src_tiles, W_dram, xT_out, ready, res_dram=None,
                  res2_dram=None, nk=8, scale=None):
    wtiles = ctx.wtiles
    _, gW = load_weight_bf16(ctx, W_dram, nk * 128, D, out=wtiles, waits=ready)
    ps = [ctx.pool["PC"], ctx.pool["PD"]]
    ysc = [ctx.scr6[0], ctx.scr6[1]]
    rb = [ctx.scr6[2], ctx.scr6[3]]
    rb2 = [ctx.scr6[4], ctx.scr6[5]]
    i = 0
    dd_prev = {}
    for m in range(8):
        for nb in range(8):
            s = slice(nb * 512, nb * 512 + 512)
            w = list(ready) + [("gs", gW)] if i < 2 else [("dv", dd_prev[i - 2])]

            def mm(t, m=m, s=s, i=i):
                for k in range(nk):
                    h = t.matmul(ps[i % 2][:, :], wtiles[k][:, m * 128:(m + 1) * 128],
                                 src_tiles[k][:, s], start=(k == 0), stop=(k == nk - 1))
                return h
            p = ctx.step("tensor", mm, waits=w)
            lr = ctx.dma("sync", rb[i % 2][:, :], res_dram[m * 128:(m + 1) * 128, s],
                         waits=w, sem_name=f"xr{i % 2}")
            if scale is not None:
                dd = ctx.step("vector", lambda v, i=i: v.tensor_scalar(
                    ysc[i % 2][:, :], ps[i % 2][:, :], scale, None, op0=ALU.mult),
                    waits=[("pe", p), (f"sy{i % 2}", ctx.cnt(f"sy{i % 2}"))])
                dd = ctx.step("vector", lambda v, i=i: v.tensor_tensor(
                    ysc[i % 2][:, :], ysc[i % 2][:, :], rb[i % 2][:, :], ALU.add),
                    waits=[("dv", dd), (f"xr{i % 2}", lr)])
            else:
                dd = ctx.step("vector", lambda v, i=i: v.tensor_tensor(
                    ysc[i % 2][:, :], ps[i % 2][:, :], rb[i % 2][:, :], ALU.add),
                    waits=[("pe", p), (f"xr{i % 2}", lr), (f"sy{i % 2}", ctx.cnt(f"sy{i % 2}"))])
            if res2_dram is not None:
                lr2 = ctx.dma("sync", rb2[i % 2][:, :], res2_dram[m * 128:(m + 1) * 128, s],
                              waits=w, sem_name=f"x2{i % 2}")
                dd = ctx.step("vector", lambda v, i=i: v.tensor_tensor(
                    ysc[i % 2][:, :], ysc[i % 2][:, :], rb2[i % 2][:, :], ALU.add),
                    waits=[("dv", dd), (f"x2{i % 2}", lr2)])
            ctx.dma("sync", xT_out[m * 128:(m + 1) * 128, s], ysc[i % 2][:, :],
                    waits=[("dv", dd)], sem_name=f"sy{i % 2}")
            dd_prev[i] = dd
            i += 1
    return [("dv", dd_prev[i - 1]), ("sy0", ctx.cnt("sy0")), ("sy1", ctx.cnt("sy1"))]




def emit_l3_attn(ctx, xT_in, xT_out, wd, C, scr):
    g = ctx.fresh("l3a")
    scr6 = ctx.scr6
    xln = ctx.xln
    ln_w = emit_layernorm(ctx, xln, C, xT_dram=xT_in, scr6=scr6)
    wtiles = ctx.wtiles

    Wq, gW = load_weight_bf16(ctx, wd["wqkvT"][:, 0:768], D, 768, out=wtiles, waits=ln_w)
    dq0, _2 = emit_proj_fm(ctx, xln, Wq, scr["qkvT"][0:768, :], ln_w + [("gs", gW)], nm=6)
    Wq, gW = load_weight_bf16(ctx, wd["wqkvT"][:, 768:1536], D, 768, out=wtiles,
                              waits=[("dv", dq0)])
    dq, _2 = emit_proj_fm(ctx, xln, Wq, scr["qkvT"][768:1536, :],
                          [("dv", dq0), ("gs", gW)], nm=6)
    Wr, gW2 = load_weight_bf16(ctx, wd["wqkredT"], D, 512, out=wtiles, waits=[("dv", dq)])
    dr, _2 = emit_proj_fm(ctx, xln, Wr, scr["qkrT"], [("dv", dq), ("gs", gW2)], nm=4)
    Wv, gW3 = load_weight_bf16(ctx, wd["wvredT"], D, 512, out=wtiles, waits=[("dv", dr)])
    dvr, _2 = emit_proj_tm(ctx, xln, Wv, scr["vred"], [("dv", dr), ("gs", gW3)], C, nout=512)
    Wvf, gW4 = load_weight_bf16(ctx, wd["wvfullT"], D, 512, out=wtiles, waits=[("dv", dvr)])
    dvf, _2 = emit_proj_tm(ctx, xln, Wvf, scr["vfull"], [("dv", dvr), ("gs", gW4)], C, nout=512)
    proj_done = [("dv", dvf), ("pfs0", ctx.cnt("pfs0")), ("pfs1", ctx.cnt("pfs1")),
                 ("pts0", ctx.cnt("pts0")), ("pts1", ctx.cnt("pts1")),
                 ("gs", ctx.cnt("gs"))]

    dif = ctx.carve("f32", 512)
    mk_done = ctx.step("gpsimd", lambda gp: gp.iota(
        dif, pattern=[[1, 512]], base=0, channel_multiplier=-1,
        allow_small_or_imprecise_dtypes=True), waits=proj_done, inc_sem="gsi",
        inc=1)
    masks = {}
    mscr = ctx.carve("f32", 512)
    mlast = ("gsi", mk_done)

    def get_mask(shift, w):
        nonlocal mlast
        key = (shift, w)
        if key in masks:
            return masks[key]
        vm = ctx.carve("bf", 512)
        d1 = ctx.step("vector", lambda v: v.tensor_scalar(
            mscr, dif, float(-shift) - 0.5, NEG, op0=ALU.is_lt, op1=ALU.mult),
            waits=[mlast])
        d1 = ctx.step("vector", lambda v: v.tensor_scalar(
            vm, dif, float(w - shift) - 0.5, None, op0=ALU.is_ge), waits=[("dv", d1)])
        d1 = ctx.step("vector", lambda v: v.scalar_tensor_tensor(
            vm, vm, NEG, mscr, op0=ALU.mult, op1=ALU.add), waits=[("dv", d1)])
        mlast = ("dv", d1)
        masks[key] = vm
        return vm

    qk = ctx.carve("bf", 8192)
    pts = [ctx.carve("bf", 512) for _ in range(2)]
    vt = [ctx.carve("bf2", 64) for _ in range(2)]
    rzb = ctx.scr6[3][:, 0:512]
    psS = [ctx.pool["PC"], ctx.pool["PD"]]
    psZ = ctx.pool["PR"][:, 0:512]
    psPV = ctx.pool["PA"]
    aout_v = scr["aoutT"]

    qkv_v = scr["qkvT"]
    qkr_v = scr["qkrT"]
    tail = mlast
    for grp in range(2):
        nh = 8
        dk = 64 if grp == 0 else 32
        win = 1024 if grp == 0 else 256
        vsrc = scr["vfull"] if grp == 0 else scr["vred"]
        for h in range(nh):
            if grp == 0:
                qrows = slice(h * 64, h * 64 + 64)
                krows = slice(512 + h * 64, 512 + h * 64 + 64)
                lq = ctx.dma("sync", qk[0:64, 0:T], qkv_v[qrows, :], waits=[tail], sem_name="qg")
                lk = ctx.dma("sync", qk[0:64, T:2 * T], qkv_v[krows, :], sem_name="qg")
            else:
                qrows = slice(h * 32, h * 32 + 32)
                krows = slice(256 + h * 32, 256 + h * 32 + 32)
                lq = ctx.dma("sync", qk[0:32, 0:T], qkr_v[qrows, :], waits=[tail], sem_name="qg")
                lk = ctx.dma("sync", qk[0:32, T:2 * T], qkr_v[krows, :], sem_name="qg")
            scale = 1.0 / math.sqrt(dk)
            for gg in range(8):
                t0 = gg * 512
                ts = slice(t0, t0 + 512)
                ilo = max(0, (t0 - win + 1) // 128)
                ihi = min(NT - 1, (t0 + 511) // 128)
                exp_done = {}
                pv_done = {}
                z_done = None
                nlist = list(range(ilo, ihi + 1))
                for j, i in enumerate(nlist):
                    shift = t0 - i * 128
                    w = [("qg", lk)] if j < 2 else [("ac", exp_done[j - 2]), ("pe", pv_done[j - 2])]

                    def mm(t, i=i, j=j, ts=ts, dk=dk, t0=t0):
                        return t.matmul(psS[j % 2][:, :], qk[0:dk, T + i * 128:T + (i + 1) * 128],
                                        qk[0:dk, t0:t0 + 512], start=True, stop=True)
                    p = ctx.step("tensor", mm, waits=w + ([("dv", ctx.cnt("dv"))] if j < 2 else []))
                    interior = (shift - 127 >= 0) and (shift + 511 < win)
                    if not interior:
                        vm = get_mask(shift, win)
                        dmsk = ctx.step("vector", lambda v, j=j, vm=vm: v.scalar_tensor_tensor(
                            psS[j % 2][:, :], psS[j % 2][:, :], scale, vm, op0=ALU.mult, op1=ALU.add),
                            waits=[("pe", p), mlast])
                        ew = [("dv", dmsk)]
                        sc_arg = 1.0
                    else:
                        ew = [("pe", p)]
                        sc_arg = scale
                    ex = ctx.step("scalar", lambda sc, j=j, sc_arg=sc_arg: sc.activation(
                        pts[j % 2][:, :], psS[j % 2][:, :], AF.Exp, scale=sc_arg),
                        waits=ew + ([("pe", pv_done[j - 2])] if j >= 2 else []))
                    exp_done[j] = ex
                    z_done = ctx.step("tensor", lambda t, j=j, n=len(nlist): t.matmul(
                        psZ, C["onesb"][:, 0:1], pts[j % 2][:, :],
                        start=(j == 0), stop=(j == n - 1)), waits=[("ac", ex)])
                    lv = ctx.dma("sync", vt[j % 2][:, :],
                                 vsrc[i * 128:(i + 1) * 128, h * 64:(h + 1) * 64],
                                 waits=w, sem_name=f"vt{j % 2}")
                    z_done = ctx.step("tensor", lambda t, j=j, n=len(nlist): t.matmul(
                        psPV[0:64, 0:512], vt[j % 2][:, 0:64], pts[j % 2][:, :],
                        start=(j == 0), stop=(j == n - 1)),
                        waits=[("ac", ex), (f"vt{j % 2}", lv)])
                    pv_done[j] = z_done
                d1 = ctx.step("vector", lambda v: v.reciprocal(ctx.zrow[:, :], psZ),
                              waits=[("pe", z_done)])
                p1 = ctx.step("tensor", lambda t: t.matmul(psS[0][:, :], C["ones"][0:1, :],
                              ctx.zrow[:, :], start=True, stop=True), waits=[("dv", d1)])
                d2 = ctx.step("vector", lambda v: v.tensor_copy(rzb, psS[0][:, :]),
                              waits=[("pe", p1)])
                d3 = ctx.step("vector", lambda v: v.tensor_tensor(
                    ctx.evbf[0][0:64, 0:512], psPV[0:64, 0:512], rzb[0:64, :], ALU.mult),
                    waits=[("dv", d2), ("sy0", ctx.cnt("sy0"))])
                row0 = grp * 512 + h * 64
                tail = ("dv", d3)
                ctx.dma("sync", aout_v[row0:row0 + 64, ts], ctx.evbf[0][0:64, 0:512],
                        waits=[("dv", d3)], sem_name="sy0")
    att_done = [tail, ("sy0", ctx.cnt("sy0"))]
    src_tiles = ctx.xln
    lds = None
    for c in range(8):
        lds = ctx.dma("gpsimd", src_tiles[c][:, :], aout_v[c * 128:(c + 1) * 128, :],
                      waits=(att_done if c == 0 else ()), sem_name="gla")
    return emit_out_proj(ctx, src_tiles, wd["wprojT"], xT_out,
                         [("gla", lds)], res_dram=xT_in)


def decl_l3_attn(nc):
    wd = {
        "wqkvT": nc.dram_tensor("l3a_wqkvT", [D, 1536], F32, kind="ExternalInput").ap(),
        "wqkredT": nc.dram_tensor("l3a_wqkredT", [D, 512], F32, kind="ExternalInput").ap(),
        "wvredT": nc.dram_tensor("l3a_wvredT", [D, 512], F32, kind="ExternalInput").ap(),
        "wvfullT": nc.dram_tensor("l3a_wvfullT", [D, 512], F32, kind="ExternalInput").ap(),
        "wprojT": nc.dram_tensor("l3a_wprojT", [D, D], F32, kind="ExternalInput").ap(),
    }
    scr = {
        "qkvT": nc.dram_tensor("l3a_qkvT", [1536, T], BF16, kind="Internal").ap(),
        "qkrT": nc.dram_tensor("l3a_qkrT", [512, T], BF16, kind="Internal").ap(),
        "vred": nc.dram_tensor("l3a_vred", [T, 512], BF16, kind="Internal").ap(),
        "vfull": nc.dram_tensor("l3a_vfull", [T, 512], BF16, kind="Internal").ap(),
        "aoutT": nc.dram_tensor("l3a_aoutT", [D, T], BF16, kind="Internal").ap(),
    }
    return wd, scr


def stage_l3_attn(inp):
    f = np.ascontiguousarray
    return {
        "l3a_wqkvT": f(inp["L3_attn_w_qkv_full"].astype(np.float32)),
        "l3a_wqkredT": f(inp["L3_attn_w_qk_red"].astype(np.float32)),
        "l3a_wvredT": f(inp["L3_attn_w_v_red"].astype(np.float32)),
        "l3a_wvfullT": f(inp["L3_attn_w_qkv_full"][:, 1024:1536].astype(np.float32)),
        "l3a_wprojT": f(inp["L3_attn_w_proj"].astype(np.float32)),
    }


EMITTERS["l3_attn"] = emit_l3_attn
DECLS["l3_attn"] = decl_l3_attn
STAGERS["l3_attn"] = stage_l3_attn




def emit_l3_ffn(ctx, xT_in, xT_out, wd, C, scr):
    g = ctx.fresh("l3f")
    scr6 = ctx.scr6
    xln = ctx.xln
    wtiles = ctx.wtiles
    ln_w = emit_layernorm(ctx, xln, C, xT_dram=xT_in, scr6=scr6, out_f32_dram=scr["x0"])
    ln_w = ln_w + [("lnw", ctx.cnt("lnw"))]
    g1c, g1d = load_cols_f32(ctx, wd["ln1"], 8)
    rms_w = emit_layernorm(ctx, xln, C, xT_dram=scr["x0"], scr6=scr6, rms=True,
                           gamma=g1c, ready=ln_w + [("dv", g1d)])

    Wq, gW = load_weight_bf16(ctx, wd["wqT"], D, D, out=wtiles, waits=rms_w)
    dq, _2 = emit_proj_fm(ctx, xln, Wq, scr["qT"], rms_w + [("gs", gW)], nm=8)
    Wk, gW = load_weight_bf16(ctx, wd["wkT"], D, 256, out=wtiles, waits=[("dv", dq)])
    dk_, _2 = emit_proj_fm(ctx, xln, Wk, scr["kT"], [("dv", dq), ("gs", gW)], nm=2)
    Wqr, gW = load_weight_bf16(ctx, wd["wqrotT"], D, D, out=wtiles, waits=[("dv", dk_)])
    dqr, _2 = emit_proj_fm(ctx, xln, Wqr, scr["qrT"], [("dv", dk_), ("gs", gW)], nm=8)
    Wkr, gW = load_weight_bf16(ctx, wd["wkrotT"], D, 256, out=wtiles, waits=[("dv", dqr)])
    dkr, _2 = emit_proj_fm(ctx, xln, Wkr, scr["krT"], [("dv", dqr), ("gs", gW)], nm=2)
    Wv, gW = load_weight_bf16(ctx, wd["wvT"], D, 256, out=wtiles, waits=[("dv", dkr)])
    dvt, _2 = emit_proj_tm(ctx, xln, Wv, scr["vtm"], [("dv", dkr), ("gs", gW)], C, nout=256)
    pd = [("dv", dvt), ("pfs0", ctx.cnt("pfs0")), ("pfs1", ctx.cnt("pfs1")),
          ("pts0", ctx.cnt("pts0")), ("pts1", ctx.cnt("pts1"))]

    cosb = ctx.carve("bf", T)
    sinb = ctx.carve("bf", T)
    lcs = ctx.dma("gpsimd", cosb, wd["cos"], waits=pd, sem_name="gla")
    lcs = ctx.dma("gpsimd", sinb, wd["sin"], sem_name="gla")
    rr = [ctx.carve("bf2", 1024), ctx.carve("bf2", 1024)]
    rscr = scr6[0]
    last = None
    for src_q, src_r, dst, nchunk in ((scr["qT"], scr["qrT"], scr["qhT"], 8),
                                      (scr["kT"], scr["krT"], scr["khT"], 2)):
        for c in range(nchunk):
            for tb in range(4):
                ts = slice(tb * 1024, tb * 1024 + 1024)
                w = ([("gla", lcs)] if last is None else [("dv", last)]) +                     [("sy0", ctx.cnt("sy0"))]
                l1 = ctx.dma("sync", rr[0][:, :], src_q[c * 128:(c + 1) * 128, ts],
                             waits=w, sem_name="kt0")
                l2 = ctx.dma("sync", rr[1][:, :], src_r[c * 128:(c + 1) * 128, ts],
                             waits=w, sem_name="kt1")
                d = ctx.step("vector", lambda v, ts=ts: v.tensor_tensor(
                    rscr[:, :], rr[1][:, 0:512], sinb[:, ts][:, 0:512], ALU.mult)
                    if False else v.tensor_tensor(
                    rscr[:, :], rr[1][:, 0:512], sinb[:, ts.start:ts.start + 512], ALU.mult),
                    waits=[("kt0", l1), ("kt1", l2)])
                for hh in range(2):
                    hs = slice(hh * 512, hh * 512 + 512)
                    gts = slice(ts.start + hh * 512, ts.start + hh * 512 + 512)
                    if hh == 0:
                        dd = d
                    else:
                        dd = ctx.step("vector", lambda v, hs=hs, gts=gts: v.tensor_tensor(
                            rscr[:, :], rr[1][:, hs], sinb[:, gts], ALU.mult),
                            waits=[("dv", last)])
                    dd = ctx.step("vector", lambda v, hs=hs, gts=gts: v.tensor_tensor(
                        rr[0][:, hs], rr[0][:, hs], cosb[:, gts], ALU.mult),
                        waits=[("dv", dd)])
                    last = ctx.step("vector", lambda v, hs=hs: v.tensor_tensor(
                        rr[0][:, hs], rr[0][:, hs], rscr[:, :], ALU.add),
                        waits=[("dv", dd)])
                lst = ctx.dma("sync", dst[c * 128:(c + 1) * 128, ts], rr[0][:, :],
                              waits=[("dv", last)], sem_name="sy0")
    rope_done = [("dv", last), ("sy0", ctx.cnt("sy0"))]

    ctx.reset_arenas()
    qk = ctx.carve("bf", 8192)
    pts = [ctx.carve("bf", 512) for _ in range(2)]
    vt = [ctx.carve("bf2", 64) for _ in range(2)]
    rzb = ctx.scr6[3][:, 0:512]
    dif = ctx.carve("f32", 512)
    mscr = ctx.carve("f32", 512)
    mk_done = ctx.step("gpsimd", lambda gp: gp.iota(
        dif, pattern=[[1, 512]], base=0, channel_multiplier=-1,
        allow_small_or_imprecise_dtypes=True), waits=rope_done, inc_sem="gsi", inc=1)
    masks = {}
    mlast = ("gsi", mk_done)

    def get_mask(shift):
        nonlocal mlast
        if shift in masks:
            return masks[shift]
        vm = ctx.carve("bf", 512)
        d1 = ctx.step("vector", lambda v: v.tensor_scalar(
            vm, dif, float(-shift) - 0.5, NEG, op0=ALU.is_lt, op1=ALU.mult), waits=[mlast])
        mlast = ("dv", d1)
        masks[shift] = vm
        return vm

    psS = [ctx.pool["PC"], ctx.pool["PD"]]
    psZ = ctx.pool["PR"][:, 0:512]
    psPV = ctx.pool["PA"]
    tail = mlast
    scale = 0.125
    for h in range(16):
        kv = h // 4
        lq = ctx.dma("sync", qk[0:64, 0:T], scr["qhT"][h * 64:(h + 1) * 64, :],
                     waits=[tail], sem_name="qg")
        lk = ctx.dma("sync", qk[0:64, T:2 * T], scr["khT"][kv * 64:(kv + 1) * 64, :],
                     sem_name="qg")
        for gg in range(8):
            t0 = gg * 512
            ni = 4 * gg + 4
            exp_done = {}
            pv_done = {}
            z_done = None
            for j in range(ni):
                i = j
                shift = t0 - i * 128
                w = [("qg", lk)] if j < 2 else [("ac", exp_done[j - 2]), ("pe", pv_done[j - 2])]

                def mm(t, i=i, j=j, t0=t0):
                    return t.matmul(psS[j % 2][:, :], qk[0:64, T + i * 128:T + (i + 1) * 128],
                                    qk[0:64, t0:t0 + 512], start=True, stop=True)
                p = ctx.step("tensor", mm, waits=w + ([("dv", ctx.cnt("dv"))] if j < 2 else []))
                if shift < 128:
                    vm = get_mask(shift)
                    dmsk = ctx.step("vector", lambda v, j=j, vm=vm: v.scalar_tensor_tensor(
                        psS[j % 2][:, :], psS[j % 2][:, :], scale, vm, op0=ALU.mult, op1=ALU.add),
                        waits=[("pe", p), mlast])
                    ew = [("dv", dmsk)]
                    sc_arg = 1.0
                else:
                    ew = [("pe", p)]
                    sc_arg = scale
                ex = ctx.step("scalar", lambda sc, j=j, sc_arg=sc_arg: sc.activation(
                    pts[j % 2][:, :], psS[j % 2][:, :], AF.Exp, scale=sc_arg),
                    waits=ew + ([("pe", pv_done[j - 2])] if j >= 2 else []))
                exp_done[j] = ex
                z_done = ctx.step("tensor", lambda t, j=j, ni=ni: t.matmul(
                    psZ, C["onesb"][:, 0:1], pts[j % 2][:, :],
                    start=(j == 0), stop=(j == ni - 1)), waits=[("ac", ex)])
                lv = ctx.dma("sync", vt[j % 2][:, :],
                             scr["vtm"][i * 128:(i + 1) * 128, kv * 64:(kv + 1) * 64],
                             waits=w, sem_name=f"vt{j % 2}")
                z_done = ctx.step("tensor", lambda t, j=j, ni=ni: t.matmul(
                    psPV[0:64, 0:512], vt[j % 2][:, 0:64], pts[j % 2][:, :],
                    start=(j == 0), stop=(j == ni - 1)),
                    waits=[("ac", ex), (f"vt{j % 2}", lv)])
                pv_done[j] = z_done
            d1 = ctx.step("vector", lambda v: v.reciprocal(ctx.zrow[:, :], psZ),
                          waits=[("pe", z_done)])
            p1 = ctx.step("tensor", lambda t: t.matmul(psS[0][:, :], C["ones"][0:1, :],
                          ctx.zrow[:, :], start=True, stop=True), waits=[("dv", d1)])
            d2 = ctx.step("vector", lambda v: v.tensor_copy(rzb, psS[0][:, :]),
                          waits=[("pe", p1)])
            d3 = ctx.step("vector", lambda v: v.tensor_tensor(
                ctx.evbf[0][0:64, 0:512], psPV[0:64, 0:512], rzb[0:64, :], ALU.mult),
                waits=[("dv", d2), ("sy0", ctx.cnt("sy0"))])
            tail = ("dv", d3)
            ctx.dma("sync", scr["aoutT"][h * 64:(h + 1) * 64, t0:t0 + 512],
                    ctx.evbf[0][0:64, 0:512], waits=[("dv", d3)], sem_name="sy0")
    att_done = [tail, ("sy0", ctx.cnt("sy0"))]
    lds = None
    for c in range(8):
        lds = ctx.dma("gpsimd", xln[c][:, :], scr["aoutT"][c * 128:(c + 1) * 128, :],
                      waits=(att_done if c == 0 else ()), sem_name="gla")
    op_done = emit_out_proj(ctx, xln, wd["woT"], scr["x1"], [("gla", lds)],
                            res_dram=scr["x0"])

    g2c, g2d = load_cols_f32(ctx, wd["ln2"], 8)
    rms2_w = emit_layernorm(ctx, xln, C, xT_dram=scr["x1"], scr6=scr6, rms=True,
                            gamma=g2c, ready=op_done + [("dv", g2d)])

    ctx.reset_arenas()
    gw_sb = ctx.evbf[1][:, 0:64]
    lgw2 = ctx.dma("gpsimd", gw_sb.rearrange("p (c e) -> p c e", c=8),
                   wd["gateT"].rearrange("(c p) e -> p c e", p=128),
                   waits=rms2_w, sem_name="gla")
    psG = ctx.pool["PC"]
    tpd = {}
    for i in range(NT):
        w = [("gla", lgw2), ("sy1", ctx.cnt("sy1"))] +             (rms2_w if i == 0 else [("dv", tpd[i - 1])])

        def mmg(t, i=i):
            for k in range(8):
                h_ = t.matmul(psG[0:128, 0:8], xln[k][:, i * 128:(i + 1) * 128],
                              gw_sb[:, k * 8:(k + 1) * 8], start=(k == 0), stop=(k == 7))
            return h_
        p = ctx.step("tensor", mmg, waits=w)
        d = ctx.step("vector", lambda v: v.tensor_reduce(scr6[2][:, 0:1], psG[0:128, 0:8],
                     axis=AX.X, op=ALU.max), waits=[("pe", p)])
        d = ctx.step("vector", lambda v: v.tensor_scalar(scr6[3][:, 0:8], psG[0:128, 0:8],
                     scr6[2][:, 0:1], None, op0=ALU.subtract), waits=[("dv", d)])
        a = ctx.step("scalar", lambda sc: sc.activation(scr6[3][:, 0:8], scr6[3][:, 0:8],
                     AF.Exp, accum_out=scr6[2][:, 1:2]), waits=[("dv", d)])
        d = ctx.step("vector", lambda v: v.reciprocal(scr6[2][:, 2:3], scr6[2][:, 1:2]),
                     waits=[("ac", a)])
        d = ctx.step("vector", lambda v: v.tensor_scalar(scr6[3][:, 0:8], scr6[3][:, 0:8],
                     scr6[2][:, 2:3], None, op0=ALU.mult), waits=[("dv", d)])
        d = ctx.step("vector", lambda v: v.max(out=scr6[4][:, 0:8], in_=scr6[3][:, 0:8]),
                     waits=[("dv", d)])
        d = ctx.step("vector", lambda v: v.memset(scr6[4][:, 2:8], NEG), waits=[("dv", d)])
        d = ctx.step("vector", lambda v: v.match_replace(out=scr6[5][:, 0:8],
                     in_to_replace=scr6[4][:, 0:8], in_values=scr6[3][:, 0:8],
                     imm_value=NEG), waits=[("dv", d)])
        d = ctx.step("vector", lambda v: v.tensor_tensor(scr6[5][:, 0:8], scr6[5][:, 0:8],
                     scr6[3][:, 0:8], ALU.not_equal), waits=[("dv", d)])
        d = ctx.step("vector", lambda v: v.tensor_tensor(scr6[5][:, 0:8], scr6[5][:, 0:8],
                     scr6[3][:, 0:8], ALU.mult), waits=[("dv", d)])
        d = ctx.step("vector", lambda v: v.tensor_reduce(scr6[2][:, 3:4], scr6[5][:, 0:8],
                     axis=AX.X, op=ALU.add), waits=[("dv", d)])
        d = ctx.step("vector", lambda v: v.reciprocal(scr6[2][:, 4:5], scr6[2][:, 3:4]),
                     waits=[("dv", d)])
        d = ctx.step("vector", lambda v: v.tensor_scalar(scr6[5][:, 0:8], scr6[5][:, 0:8],
                     scr6[2][:, 4:5], None, op0=ALU.mult), waits=[("dv", d)])
        p2 = ctx.step("tensor", lambda t: t.transpose(
            ctx.pool["PB"][0:8, 0:128], scr6[5][:, 0:8], C["ident"][:, :]), waits=[("dv", d)])
        d = ctx.step("vector", lambda v: v.tensor_copy(scr6[2][0:8, 0:128],
                     ctx.pool["PB"][0:8, 0:128]), waits=[("pe", p2)])
        tpd[i] = d
        ctx.dma("sync", scr["wf"][:, i * 128:(i + 1) * 128], scr6[2][0:8, 0:128],
                waits=[("dv", d)], sem_name="sy1")
    gate_done = [("dv", tpd[NT - 1]), ("sy1", ctx.cnt("sy1"))]

    wb1 = [ctx.carve("bf2", 1024), ctx.carve("bf2", 1024)]
    wb3 = [ctx.evbf[0][:, 0:1024], ctx.evbf[1][:, 0:1024]]
    w2buf = [ctx.carve("bf2", 2048), ctx._ar["i32"].bitcast(BF16)[:, 0:2048]]
    gbuf = [ctx.carve("bf", 1024) for _ in range(16)]
    webc = ctx._ar["f32b"]
    silu_s = scr6[0]
    psA, psB = ctx.pool["PA"], ctx.pool["PB"]
    psY = [ctx.pool["PC"], ctx.pool["PD"]]
    w13T_v = wd["w13T"].rearrange("e (c p) f -> e p c f", p=128)
    w2T_v = wd["w2T"].rearrange("e (c p) m -> e p c m", p=128)
    last = gate_done
    for e in range(8):
        dwe = None
        for tb in range(8):
            lwe = ctx.dma("sync", ctx.zrow[:, 0:512],
                          scr["wf"][e, tb * 512:(tb + 1) * 512].rearrange("(o f) -> o f", o=1),
                          waits=(last if tb == 0 else [("dv", dwe)]), sem_name="lnr0")
            pb = ctx.step("tensor", lambda t: t.matmul(
                psA[:, 0:512], C["ones"][0:1, :], ctx.zrow[:, 0:512],
                start=True, stop=True), waits=[("lnr0", lwe)])
            dwe = ctx.step("vector", lambda v, tb=tb: v.tensor_copy(
                webc[:, tb * 512:(tb + 1) * 512], psA[:, 0:512]), waits=[("pe", pb)])
        we_done = dwe
        for tq in range(4):
            ts = slice(tq * 1024, tq * 1024 + 1024)
            gdone = {}
            for fc in range(16):
                if fc == 0 and tq == 0:
                    wld = [("dv", we_done)]
                elif fc >= 2:
                    wld = [("dv", gdone[fc - 2])]
                elif fc == 1:
                    wld = []
                else:
                    wld = [("dv", gdone16)]
                l1 = ctx.dma("gpsimd", wb1[fc % 2].rearrange("p (c f) -> p c f", c=8),
                             w13T_v[e, :, :, fc * 128:(fc + 1) * 128],
                             waits=wld, sem_name=f"wf{fc % 2}")
                l3 = ctx.dma("gpsimd", wb3[fc % 2].rearrange("p (c f) -> p c f", c=8),
                             w13T_v[e, :, :, 2048 + fc * 128:2048 + (fc + 1) * 128],
                             sem_name=f"wf{2 + fc % 2}")

                def mm1(t, fc=fc, ts=ts):
                    for half in range(2):
                        hs = slice(ts.start + half * 512, ts.start + half * 512 + 512)
                        for k in range(8):
                            t.matmul(psA[:, half * 512:half * 512 + 512],
                                     wb1[fc % 2][:, k * 128:(k + 1) * 128], xln[k][:, hs],
                                     start=(k == 0), stop=(k == 7))
                    for half in range(2):
                        hs = slice(ts.start + half * 512, ts.start + half * 512 + 512)
                        for k in range(8):
                            h_ = t.matmul(psB[:, half * 512:half * 512 + 512],
                                          wb3[fc % 2][:, k * 128:(k + 1) * 128], xln[k][:, hs],
                                          start=(k == 0), stop=(k == 7))
                    return h_
                p = ctx.step("tensor", mm1,
                             waits=[(f"wf{fc % 2}", l1), (f"wf{2 + fc % 2}", l3)] +
                             ([("dv", gdone[fc - 1])] if fc >= 1 else [("dv", we_done)]))
                dd = None
                for half in range(2):
                    a = ctx.step("scalar", lambda sc, half=half: sc.activation(
                        silu_s[:, :], psA[:, half * 512:half * 512 + 512], AF.Silu),
                        waits=[("pe", p)] if half == 0 else [("dv", dd)])
                    dd = ctx.step("vector", lambda v, half=half, fc=fc: v.tensor_tensor(
                        gbuf[fc][:, half * 512:half * 512 + 512], silu_s[:, :],
                        psB[:, half * 512:half * 512 + 512], ALU.mult), waits=[("ac", a)])
                    dd = ctx.step("vector", lambda v, half=half, fc=fc, ts=ts: v.tensor_tensor(
                        gbuf[fc][:, half * 512:half * 512 + 512],
                        gbuf[fc][:, half * 512:half * 512 + 512],
                        webc[:, ts.start + half * 512:ts.start + half * 512 + 512],
                        ALU.mult), waits=[("dv", dd)])
                gdone[fc] = dd
            gdone16 = gdone[15]
            mm2p = {}
            for m in range(8):
                lw2 = ctx.dma("gpsimd", w2buf[m % 2].rearrange("p (c m) -> p c m", c=16),
                              w2T_v[e, :, :, m * 128:(m + 1) * 128],
                              waits=([("dv", gdone[15])] if m < 2 else
                                     [("pe", mm2p[m - 2])]), sem_name=f"wp{m % 2}")

                def mm2(t, m=m):
                    for half in range(2):
                        for fc in range(16):
                            h_ = t.matmul(psY[half][:, 0:512],
                                          w2buf[m % 2][:, fc * 128:(fc + 1) * 128],
                                          gbuf[fc][:, half * 512:half * 512 + 512],
                                          start=(fc == 0), stop=(fc == 15))
                    return h_
                p = ctx.step("tensor", mm2,
                             waits=[(f"wp{m % 2}", lw2), ("dv", gdone[15])])
                mm2p[m] = p
                dd2 = None
                for half in range(2):
                    dd2 = ctx.step("vector", lambda v, half=half: v.tensor_copy(
                        scr6[4 + half][:, :], psY[half][:, 0:512]),
                        waits=[("pe", p), (f"ya{half}", ctx.cnt(f"ya{half}"))])
                    ctx.dma("gpsimd", scr["ye"][m * 128:(m + 1) * 128,
                            ts.start + half * 512:ts.start + half * 512 + 512],
                            scr6[4 + half][:, :], waits=[("dv", dd2)],
                            sem_name=f"ya{half}",
                            accum_op=(ALU.add if e > 0 else ALU.bypass))
                last_g = dd2
            last = [("dv", last_g)]
    moe_done = [("dv", last_g), ("ya0", ctx.cnt("ya0")), ("ya1", ctx.cnt("ya1"))]

    fin = None
    for c in range(8):
        for tb in range(8):
            s = slice(tb * 512, (tb + 1) * 512)
            i = 8 * c + tb
            w = moe_done if i == 0 else [("dv", fin)]
            l1 = ctx.dma("sync", scr6[0][:, :], scr["ye"][c * 128:(c + 1) * 128, s],
                         waits=w, sem_name="kt0")
            l2 = ctx.dma("sync", scr6[1][:, :], scr["x1"][c * 128:(c + 1) * 128, s],
                         waits=w, sem_name="kt1")
            l3 = ctx.dma("sync", scr6[2][:, :], xT_in[c * 128:(c + 1) * 128, s],
                         waits=w, sem_name="qg")
            dd = ctx.step("vector", lambda v: v.tensor_tensor(
                scr6[3][:, :], scr6[0][:, :], scr6[1][:, :], ALU.add),
                waits=[("kt0", l1), ("kt1", l2), ("sy0", ctx.cnt("sy0"))])
            fin = ctx.step("vector", lambda v: v.tensor_tensor(
                scr6[3][:, :], scr6[3][:, :], scr6[2][:, :], ALU.add),
                waits=[("dv", dd), ("qg", l3)])
            ctx.dma("sync", xT_out[c * 128:(c + 1) * 128, s], scr6[3][:, :],
                    waits=[("dv", fin)], sem_name="sy0")
    return ctx.cnt("sy0")


def decl_l3_ffn(nc):
    wd = {
        "wqT": nc.dram_tensor("l3f_wqT", [D, D], F32, kind="ExternalInput").ap(),
        "wkT": nc.dram_tensor("l3f_wkT", [D, 256], F32, kind="ExternalInput").ap(),
        "wqrotT": nc.dram_tensor("l3f_wqrotT", [D, D], F32, kind="ExternalInput").ap(),
        "wkrotT": nc.dram_tensor("l3f_wkrotT", [D, 256], F32, kind="ExternalInput").ap(),
        "wvT": nc.dram_tensor("l3f_wvT", [D, 256], F32, kind="ExternalInput").ap(),
        "woT": nc.dram_tensor("l3f_woT", [D, D], F32, kind="ExternalInput").ap(),
        "ln1": nc.dram_tensor("l3f_ln1", [D], F32, kind="ExternalInput").ap(),
        "ln2": nc.dram_tensor("l3f_ln2", [D], F32, kind="ExternalInput").ap(),
        "gateT": nc.dram_tensor("l3f_gateT", [D, 8], F32, kind="ExternalInput").ap(),
        "cos": nc.dram_tensor("l3f_cos", [128, T], F32, kind="ExternalInput").ap(),
        "sin": nc.dram_tensor("l3f_sin", [128, T], F32, kind="ExternalInput").ap(),
        "w13T": nc.dram_tensor("l3f_w13T", [8, D, 4096], F32, kind="ExternalInput").ap(),
        "w2T": nc.dram_tensor("l3f_w2T", [8, 2048, D], F32, kind="ExternalInput").ap(),
    }
    scr = {
        "x0": nc.dram_tensor("l3f_x0", [D, T], F32, kind="Internal").ap(),
        "x1": nc.dram_tensor("l3f_x1", [D, T], F32, kind="Internal").ap(),
        "qT": nc.dram_tensor("l3f_qT", [D, T], BF16, kind="Internal").ap(),
        "kT": nc.dram_tensor("l3f_kT", [256, T], BF16, kind="Internal").ap(),
        "qrT": nc.dram_tensor("l3f_qrT", [D, T], BF16, kind="Internal").ap(),
        "krT": nc.dram_tensor("l3f_krT", [256, T], BF16, kind="Internal").ap(),
        "qhT": nc.dram_tensor("l3f_qhT", [D, T], BF16, kind="Internal").ap(),
        "khT": nc.dram_tensor("l3f_khT", [256, T], BF16, kind="Internal").ap(),
        "vtm": nc.dram_tensor("l3f_vtm", [T, 256], BF16, kind="Internal").ap(),
        "aoutT": nc.dram_tensor("l3f_aoutT", [D, T], BF16, kind="Internal").ap(),
        "wf": nc.dram_tensor("l3f_wf", [8, T], F32, kind="Internal").ap(),
        "ye": nc.dram_tensor("l3f_ye", [D, T], F32, kind="Internal").ap(),
    }
    return wd, scr


def stage_l3_ffn(inp):
    f = np.ascontiguousarray
    wqkv = inp["L3_ffn_wqkv"].astype(np.float32)
    wT = wqkv.T
    HD = 64
    qcols = wT[:, 0:1024]
    kcols = wT[:, 1024:1280]
    perm = np.arange(1024).reshape(16, 64)
    perm = np.concatenate([perm[:, 32:64], perm[:, 0:32]], axis=1).reshape(-1)
    permk = np.arange(256).reshape(4, 64)
    permk = np.concatenate([permk[:, 32:64], permk[:, 0:32]], axis=1).reshape(-1)
    HDi = np.arange(64)
    inv = 1.0 / 10000.0 ** (np.arange(0, 64, 2, dtype=np.float64) / 64)
    freqs = np.arange(T, dtype=np.float64)[:, None] * inv[None, :]
    emb = np.concatenate([freqs, freqs], axis=1)
    cos = np.cos(emb)
    sin = np.sin(emb)
    sgn = np.where(HDi < 32, -1.0, 1.0)
    cos_fm = np.zeros((128, T), np.float32)
    sin_fm = np.zeros((128, T), np.float32)
    for p in range(128):
        cos_fm[p] = cos[:, p % 64]
        sin_fm[p] = sin[:, p % 64] * sgn[p % 64]
    return {
        "l3f_wqT": f(qcols),
        "l3f_wkT": f(kcols),
        "l3f_wqrotT": f(qcols[:, perm]),
        "l3f_wkrotT": f(kcols[:, permk]),
        "l3f_wvT": f(wT[:, 1280:1536]),
        "l3f_woT": f(inp["L3_ffn_wo"].astype(np.float32).T),
        "l3f_ln1": f(inp["L3_ffn_ln1_w"].astype(np.float32)),
        "l3f_ln2": f(inp["L3_ffn_ln2_w"].astype(np.float32)),
        "l3f_gateT": f(inp["L3_ffn_gate_w"].astype(np.float32).T),
        "l3f_cos": cos_fm,
        "l3f_sin": sin_fm,
        "l3f_w13T": f(inp["L3_ffn_w13"].astype(np.float32).transpose(0, 2, 1)),
        "l3f_w2T": f(inp["L3_ffn_w2"].astype(np.float32).transpose(0, 2, 1)),
    }


EMITTERS["l3_ffn"] = emit_l3_ffn
DECLS["l3_ffn"] = decl_l3_ffn
STAGERS["l3_ffn"] = stage_l3_ffn




def emit_l2_attn(ctx, xT_in, xT_out, wd, C, scr):
    g = ctx.fresh("l2a")
    scr6 = ctx.scr6
    xln = ctx.xln
    wtiles = ctx.wtiles
    ln_w = emit_layernorm(ctx, xln, C, xT_dram=xT_in, scr6=scr6)

    Wq, gW = load_weight_bf16(ctx, wd["wqT"], D, D, out=wtiles, waits=ln_w)
    dq, _2 = emit_proj_tm(ctx, xln, Wq, scr["qtm"], ln_w + [("gs", gW)], C)
    Wk, gW = load_weight_bf16(ctx, wd["wkT"], D, D, out=wtiles, waits=[("dv", dq)])
    dk_, _2 = emit_proj_tm(ctx, xln, Wk, scr["ktm"], [("dv", dq), ("gs", gW)], C)
    Wv, gW = load_weight_bf16(ctx, wd["wvT"], D, D, out=wtiles, waits=[("dv", dk_)])
    dvt, _2 = emit_proj_tm(ctx, xln, Wv, scr["vtm"], [("dv", dk_), ("gs", gW)], C)
    pd = [("dv", dvt), ("pts0", ctx.cnt("pts0")), ("pts1", ctx.cnt("pts1")),
          ("pfs0", ctx.cnt("pfs0")), ("pfs1", ctx.cnt("pfs1"))]

    cs = ctx.carve("f32", 128)
    rin = ctx.carve("bf2", 1024)
    rot = ctx.carve("f32", 1024)
    nrm = ctx.carve("f32", 1024)
    psT_ = [ctx.pool["PC"], ctx.pool["PD"]]
    last = None
    for which, src_d, dst_d in ((0, scr["qtm"], scr["qhT"]), (1, scr["ktm"], scr["khT"])):
        for i in range(NT):
            w = pd + [("cst", ctx.cnt("cst"))] if last is None else [("dv", last)]
            lcs = ctx.dma("sync", cs.rearrange("p (a d) -> p a d", a=2),
                          wd["cs"][i * 128:(i + 1) * 128, :, :], waits=w, sem_name="kt0")
            lx = ctx.dma("sync", rin[:, :], src_d[i * 128:(i + 1) * 128, :],
                         waits=w, sem_name="kt1")
            dd = ctx.step("vector", lambda v: v.tensor_scalar(
                rot.rearrange("p (h d two) -> p (h d) two", two=2, h=16)[:, :, 0:1].rearrange("p a b -> p (a b)"),
                rin.rearrange("p (h d two) -> p (h d) two", two=2, h=16)[:, :, 1:2].rearrange("p a b -> p (a b)"),
                -1.0, None, op0=ALU.mult),
                waits=[("kt0", lcs), ("kt1", lx)])
            dd = ctx.step("vector", lambda v: v.tensor_copy(
                rot.rearrange("p (h d two) -> p (h d) two", two=2, h=16)[:, :, 1:2].rearrange("p a b -> p (a b)"),
                rin.rearrange("p (h d two) -> p (h d) two", two=2, h=16)[:, :, 0:1].rearrange("p a b -> p (a b)")),
                waits=[("dv", dd)])
            dd = ctx.step("vector", lambda v: v.tensor_tensor(
                rot.rearrange("p (h d) -> p h d", h=16), rot.rearrange("p (h d) -> p h d", h=16),
                cs[:, 64:128].rearrange("p (o d) -> p o d", o=1).to_broadcast([128, 16, 64]),
                ALU.mult), waits=[("dv", dd)])
            dd = ctx.step("vector", lambda v: v.scalar_tensor_tensor(
                nrm.rearrange("p (h d) -> p h d", h=16), rin.rearrange("p (h d) -> p h d", h=16), 1.0,
                cs[:, 0:64].rearrange("p (o d) -> p o d", o=1).to_broadcast([128, 16, 64]),
                op0=ALU.mult, op1=ALU.mult), waits=[("dv", dd)])
            dd = ctx.step("vector", lambda v: v.tensor_tensor(
                nrm[:, :], nrm[:, :], rot[:, :], ALU.add), waits=[("dv", dd)])
            dd2 = ctx.step("vector", lambda v: v.tensor_tensor(
                rot[:, :], nrm[:, :], nrm[:, :], ALU.mult), waits=[("dv", dd)])
            dd2 = ctx.step("vector", lambda v: v.tensor_reduce(
                scr6[0][:, 0:16], rot.rearrange("p (h d) -> p h d", h=16),
                axis=AX.X, op=ALU.add), waits=[("dv", dd2)])
            a = ctx.step("scalar", lambda sc: sc.activation(
                scr6[0][:, 16:32], scr6[0][:, 0:16], AF.Sqrt), waits=[("dv", dd2)])
            dd2 = ctx.step("vector", lambda v: v.tensor_scalar(
                scr6[0][:, 16:32], scr6[0][:, 16:32], 1e-12, None, op0=ALU.max),
                waits=[("ac", a)])
            dd2 = ctx.step("vector", lambda v: v.reciprocal(
                scr6[0][:, 32:48], scr6[0][:, 16:32]), waits=[("dv", dd2)])
            if which == 0:
                dd2 = ctx.step("vector", lambda v: v.tensor_scalar(
                    scr6[0][:, 32:48], scr6[0][:, 32:48], 0.125, None, op0=ALU.mult),
                    waits=[("dv", dd2)])
            dd2 = ctx.step("vector", lambda v: v.tensor_tensor(
                nrm.rearrange("p (h d) -> p h d", h=16), nrm.rearrange("p (h d) -> p h d", h=16),
                scr6[0][:, 32:48].rearrange("p (h o) -> p h o", o=1).to_broadcast([128, 16, 64]),
                ALU.mult), waits=[("dv", dd2)])
            dd2 = ctx.step("vector", lambda v: v.tensor_copy(rin[:, :], nrm[:, :]),
                          waits=[("dv", dd2)])
            for c8 in range(8):
                p_ = ctx.step("tensor", lambda t, c8=c8: t.transpose(
                    psT_[c8 % 2].bitcast(BF16)[:, 0:128] if False else psT_[c8 % 2][:, 0:64].bitcast(BF16) if False else psT_[c8 % 2][:, 0:128],
                    rin[:, c8 * 128:(c8 + 1) * 128], C["ident"][:, :]) if False else t.transpose(
                    ctx.pool["PA"][0:128, 0:128] if False else psT_[c8 % 2][:, 0:128],
                    nrm[:, c8 * 128:(c8 + 1) * 128], C["ident"][:, :]),
                    waits=[("dv", dd2)] if c8 < 2 else [("dv", last)])
                last = ctx.step("vector", lambda v, c8=c8: v.tensor_copy(
                    ctx.evbf[c8 % 2][:, 0:128], psT_[c8 % 2][:, 0:128]),
                    waits=[("pe", p_), (f"sy{c8 % 2}", ctx.cnt(f"sy{c8 % 2}"))])
                ctx.dma("sync", dst_d[c8 * 128:(c8 + 1) * 128, i * 128:(i + 1) * 128],
                        ctx.evbf[c8 % 2][:, 0:128], waits=[("dv", last)],
                        sem_name=f"sy{c8 % 2}")
    rope_done = [("dv", last), ("sy0", ctx.cnt("sy0")), ("sy1", ctx.cnt("sy1"))]

    _s1 = ctx.nc.enter_named_scope("l2a_p1", False)[0]
    ctx.reset_arenas()
    qk = ctx.carve("bf", 8192)
    bufB = ctx._ar["f32b"]
    bufA = ctx._ar["f32"][:, 0:3072]
    m16A = ctx.scr6[1][:, 0:8]
    m16B = ctx.scr6[1][:, 8:16]
    dif = ctx.scr6[2][:, 0:512]
    psS = [ctx.pool["PC"], ctx.pool["PD"]]
    mk = ctx.step("gpsimd", lambda gp: gp.iota(
        dif, pattern=[[1, 512]], base=0, channel_multiplier=-1,
        allow_small_or_imprecise_dtypes=True), waits=rope_done, inc_sem="gsi", inc=1)
    vm0 = ctx.carve("bf", 512)
    dmk = ctx.step("vector", lambda v: v.tensor_scalar(
        vm0, dif, -0.5, NEG, op0=ALU.is_lt, op1=ALU.mult), waits=[("gsi", mk)])
    vmfull = []
    for k in range(4):
        vmk = ctx.carve("bf", 512)
        dmk = ctx.step("vector", lambda v, k=k, vmk=vmk: v.tensor_scalar(
            vmk, dif, 128.0 * k - 0.5, NEG, op0=ALU.is_lt, op1=ALU.mult),
            waits=[("dv", dmk)])
        vmfull.append(vmk)
    tailm = ("dv", dmk)
    tail = ("dv", dmk)
    qv = scr["qhT"].rearrange("(c p) t -> p c t", p=128)
    kv = scr["khT"].rearrange("(c p) t -> p c t", p=128)

    def emit_scores(i, buf, lk, first_w, evprev):
        nb = i // 4 + 1
        ev = dict(evprev)
        base = len(evprev)
        for b in range(nb):
            p = ctx.step("tensor", lambda t, i=i, b=b: t.matmul(
                psS[(base + b) % 2][:, :], qk[0:64, 0:T][:, i * 128:i * 128 + 128],
                qk[0:64, T + b * 512:T + (b + 1) * 512], start=True, stop=True),
                waits=(([("qg", lk)] if lk is not None and b == 0 else []) +
                       (first_w if b < 2 else []) +
                       ([("dv", ev[base + b - 2])] if base + b >= 2 else [])))
            if b == nb - 1:
                sh = i * 128 - b * 512
                dd = None
                for st in range(4):
                    if 128 * st > sh:
                        dd = ctx.step("vector", lambda v, b=b, st=st, buf=buf: v.memset(
                            buf[:, b * 512 + st * 128: b * 512 + (st + 1) * 128], NEG),
                            waits=[("pe", p)] if dd is None else [("dv", dd)])
                    elif 128 * st == sh:
                        dd = ctx.step("vector", lambda v, b=b, st=st, buf=buf:
                            v.scalar_tensor_tensor(
                            buf[:, b * 512 + st * 128: b * 512 + (st + 1) * 128],
                            psS[(base + b) % 2][:, st * 128:(st + 1) * 128], 1.0,
                            vm0[:, 0:128], op0=ALU.mult, op1=ALU.add),
                            waits=([("pe", p)] if dd is None else [("dv", dd)]) + [tailm])
                    else:
                        dd = ctx.step("vector", lambda v, b=b, st=st, buf=buf: v.tensor_copy(
                            buf[:, b * 512 + st * 128: b * 512 + (st + 1) * 128],
                            psS[(base + b) % 2][:, st * 128:(st + 1) * 128]),
                            waits=[("pe", p)] if dd is None else [("dv", dd)])
                ev[base + b] = dd
            else:
                ev[base + b] = ctx.step("vector", lambda v, b=b, buf=buf: v.tensor_copy(
                    buf[:, b * 512:(b + 1) * 512], psS[(base + b) % 2][:, :]),
                    waits=[("pe", p)])
        return ev

    for h in range(16):
        hc, ho = h // 2, (h % 2) * 64
        lq = ctx.dma("sync", qk[0:64, 0:T], qv[ho:ho + 64, hc, :], waits=[tail], sem_name="qg")
        lk = ctx.dma("sync", qk[0:64, T:2 * T], kv[ho:ho + 64, hc, :], sem_name="qg")
        for pi in range(16):
            iA, iB = pi, 16 + pi
            LA, LB = (iA // 4 + 1) * 512, (iB // 4 + 1) * 512
            ev = emit_scores(iA, bufA, lk if pi == 0 else None, [tail], {})
            ev = emit_scores(iB, bufB, None, [tail], ev)
            dA, dB = ev[len(ev) - 1], ev[len(ev) - 1]
            for r in range(16):
                dA = ctx.step("vector", lambda v, LA=LA: v.max(
                    out=m16A, in_=bufA[:, 0:LA]), waits=[("dv", dA)])
                dB = ctx.step("vector", lambda v, LB=LB: v.max(
                    out=m16B, in_=bufB[:, 0:LB]), waits=[("dv", dB)])
                if r < 15:
                    dA = ctx.step("vector", lambda v, LA=LA: v.match_replace(
                        out=bufA[:, 0:LA], in_to_replace=m16A,
                        in_values=bufA[:, 0:LA], imm_value=NEG), waits=[("dv", dA)])
                    dB = ctx.step("vector", lambda v, LB=LB: v.match_replace(
                        out=bufB[:, 0:LB], in_to_replace=m16B,
                        in_values=bufB[:, 0:LB], imm_value=NEG), waits=[("dv", dB)])
            pA = ctx.step("tensor", lambda t: t.transpose(
                ctx.pool["PA"][0:1, 0:128], m16A[:, 7:8], C["ident"][:, :]),
                waits=[("dv", dA), ("dv", dB)])
            dA2 = ctx.step("vector", lambda v: v.tensor_copy(
                ctx.zrow[0:1, 0:128], ctx.pool["PA"][0:1, 0:128]),
                waits=[("pe", pA), ("sy0", ctx.cnt("sy0"))])
            pB = ctx.step("tensor", lambda t: t.transpose(
                ctx.pool["PA"][0:1, 128:256], m16B[:, 7:8], C["ident"][:, :]),
                waits=[("dv", dA2)])
            dB2 = ctx.step("vector", lambda v: v.tensor_copy(
                ctx.zrow[0:1, 128:256], ctx.pool["PA"][0:1, 128:256]),
                waits=[("pe", pB)])
            ctx.dma("sync", scr["th"][h, iA * 128:(iA + 1) * 128].rearrange("(o f) -> o f", o=1),
                    ctx.zrow[0:1, 0:128], waits=[("dv", dB2)], sem_name="sy0")
            ctx.dma("sync", scr["th"][h, iB * 128:(iB + 1) * 128].rearrange("(o f) -> o f", o=1),
                    ctx.zrow[0:1, 128:256], waits=[("dv", dB2)], sem_name="sy0")
            tail = ("dv", dB2)
    th_done = [tail, ("sy0", ctx.cnt("sy0"))]
    ctx.nc.leave_named_scope("l2a_p1", _s1, False)
    _s2 = ctx.nc.enter_named_scope("l2a_p2", False)[0]

    pts = [ctx.carve("bf", 512) for _ in range(2)]
    vt = [ctx.carve("bf2", 64) for _ in range(2)]
    rzb = ctx.scr6[3][:, 0:512]
    thbc = bufB
    psZ = ctx.pool["PR"][:, 0:512]
    psPV = ctx.pool["PA"]
    tail2 = th_done
    for h in range(16):
        hc, ho = h // 2, (h % 2) * 64
        lq = ctx.dma("sync", qk[0:64, 0:T], qv[ho:ho + 64, hc, :], waits=tail2, sem_name="qg")
        lk = ctx.dma("sync", qk[0:64, T:2 * T], kv[ho:ho + 64, hc, :], sem_name="qg")
        dth = None
        for tb in range(8):
            lth = ctx.dma("sync", ctx.zrow[:, 0:512],
                          scr["th"][h, tb * 512:(tb + 1) * 512].rearrange("(o f) -> o f", o=1),
                          waits=(tail2 if tb == 0 else [("dv", dth)]), sem_name="lnr0")
            pb = ctx.step("tensor", lambda t: t.matmul(
                ctx.pool["PB"][:, 0:512], C["ones"][0:1, :], ctx.zrow[:, 0:512],
                start=True, stop=True), waits=[("lnr0", lth)])
            dth = ctx.step("vector", lambda v, tb=tb: v.tensor_copy(
                thbc[:, tb * 512:(tb + 1) * 512], ctx.pool["PB"][:, 0:512]),
                waits=[("pe", pb)])
        for gg in range(8):
            t0 = gg * 512
            ni = 4 * gg + 4
            exp_done = {}
            pv_done = {}
            z_done = None
            for j in range(ni):
                i = j
                shift = t0 - i * 128
                w = [("qg", lk), ("dv", dth)] if j < 2 else [("ac", exp_done[j - 2]), ("pe", pv_done[j - 2])]

                def mm(t, i=i, j=j, t0=t0):
                    return t.matmul(psS[j % 2][:, :], qk[0:64, T + i * 128:T + (i + 1) * 128],
                                    qk[0:64, t0:t0 + 512], start=True, stop=True)
                p = ctx.step("tensor", mm, waits=w + ([("dv", ctx.cnt("dv"))] if j < 2 else []))
                dd = None
                if i >= 4 * gg:
                    kk = i - 4 * gg
                    dd = ctx.step("vector", lambda v, j=j, kk=kk: v.scalar_tensor_tensor(
                        psS[j % 2][:, :], psS[j % 2][:, :], 1.0, vmfull[kk][:, :],
                        op0=ALU.mult, op1=ALU.add), waits=[("pe", p), tailm])
                dd = ctx.step("vector", lambda v, j=j, t0=t0: v.tensor_tensor(
                    scr6[0][:, :], psS[j % 2][:, :], thbc[:, t0:t0 + 512], ALU.is_ge),
                    waits=([("dv", dd)] if dd is not None else [("pe", p)]) +
                    ([("ac", exp_done[j - 2])] if j >= 2 else []))
                dd = ctx.step("vector", lambda v, j=j: v.scalar_tensor_tensor(
                    psS[j % 2][:, :], scr6[0][:, :], 200.0, psS[j % 2][:, :],
                    op0=ALU.mult, op1=ALU.add), waits=[("dv", dd)])
                ex = ctx.step("scalar", lambda sc, j=j: sc.activation(
                    pts[j % 2][:, :], psS[j % 2][:, :], AF.Exp, bias=C["neg200"][:, 0:1], scale=1.0),
                    waits=[("dv", dd), ("pe", pv_done[j - 2])] if j >= 2 else [("dv", dd)])
                exp_done[j] = ex
                z_done = ctx.step("tensor", lambda t, j=j, ni=ni: t.matmul(
                    psZ, C["onesb"][:, 0:1], pts[j % 2][:, :],
                    start=(j == 0), stop=(j == ni - 1)), waits=[("ac", ex)])
                lv = ctx.dma("sync", vt[j % 2][:, :],
                             scr["vtm"][i * 128:(i + 1) * 128, h * 64:(h + 1) * 64],
                             waits=w, sem_name=f"vt{j % 2}")
                z_done = ctx.step("tensor", lambda t, j=j, ni=ni: t.matmul(
                    psPV[0:64, 0:512], vt[j % 2][:, 0:64], pts[j % 2][:, :],
                    start=(j == 0), stop=(j == ni - 1)),
                    waits=[("ac", ex), (f"vt{j % 2}", lv)])
                pv_done[j] = z_done
            d1 = ctx.step("vector", lambda v: v.reciprocal(ctx.zrow[:, :], psZ),
                          waits=[("pe", z_done)])
            p1 = ctx.step("tensor", lambda t: t.matmul(psS[0][:, :], C["ones"][0:1, :],
                          ctx.zrow[:, :], start=True, stop=True), waits=[("dv", d1)])
            d2 = ctx.step("vector", lambda v: v.tensor_copy(rzb, psS[0][:, :]),
                          waits=[("pe", p1)])
            d3 = ctx.step("vector", lambda v: v.tensor_tensor(
                ctx.evbf[0][0:64, 0:512], psPV[0:64, 0:512], rzb[0:64, :], ALU.mult),
                waits=[("dv", d2), ("sy1", ctx.cnt("sy1"))])
            tail2 = [("dv", d3)]
            ctx.dma("sync", scr["aoutT"][h * 64:(h + 1) * 64, t0:t0 + 512],
                    ctx.evbf[0][0:64, 0:512], waits=[("dv", d3)], sem_name="sy1")
    att_done = tail2 + [("sy1", ctx.cnt("sy1"))]
    ctx.nc.leave_named_scope("l2a_p2", _s2, False)
    lds = None
    for c in range(8):
        lds = ctx.dma("gpsimd", xln[c][:, :], scr["aoutT"][c * 128:(c + 1) * 128, :],
                      waits=(att_done if c == 0 else ()), sem_name="gla")
    return emit_out_proj(ctx, xln, wd["wprojT"], xT_out, [("gla", lds)], res_dram=xT_in)


def decl_l2_attn(nc):
    wd = {
        "wqT": nc.dram_tensor("l2a_wqT", [D, D], F32, kind="ExternalInput").ap(),
        "wkT": nc.dram_tensor("l2a_wkT", [D, D], F32, kind="ExternalInput").ap(),
        "wvT": nc.dram_tensor("l2a_wvT", [D, D], F32, kind="ExternalInput").ap(),
        "wprojT": nc.dram_tensor("l2a_wprojT", [D, D], F32, kind="ExternalInput").ap(),
        "cs": nc.dram_tensor("l2a_cs", [T, 2, 64], F32, kind="ExternalInput").ap(),
    }
    scr = {
        "qtm": nc.dram_tensor("l2a_qtm", [T, D], BF16, kind="Internal").ap(),
        "ktm": nc.dram_tensor("l2a_ktm", [T, D], BF16, kind="Internal").ap(),
        "vtm": nc.dram_tensor("l2a_vtm", [T, D], BF16, kind="Internal").ap(),
        "qhT": nc.dram_tensor("l2a_qhT", [D, T], BF16, kind="Internal").ap(),
        "khT": nc.dram_tensor("l2a_khT", [D, T], BF16, kind="Internal").ap(),
        "th": nc.dram_tensor("l2a_th", [16, T], F32, kind="Internal").ap(),
        "aoutT": nc.dram_tensor("l2a_aoutT", [D, T], BF16, kind="Internal").ap(),
    }
    return wd, scr


def stage_l2_attn(inp):
    f = np.ascontiguousarray
    wqkv = inp["L2_attn_w_qkv"].astype(np.float32)
    wT = wqkv.T
    inv = 1.0 / 10000.0 ** (np.arange(0, 64, 2, dtype=np.float64) / 64)
    t_ = np.arange(T, dtype=np.float64)
    freqs = t_[:, None] * inv[None, :]
    emb = np.repeat(freqs, 2, axis=1)
    cs = np.stack([np.cos(emb), np.sin(emb)], axis=1).astype(np.float32)
    return {
        "l2a_wqT": f(wT[:, 0:1024]),
        "l2a_wkT": f(wT[:, 1024:2048]),
        "l2a_wvT": f(wT[:, 2048:3072]),
        "l2a_wprojT": f(inp["L2_attn_w_proj"].astype(np.float32).T),
        "l2a_cs": f(cs),
    }


EMITTERS["l2_attn"] = emit_l2_attn
DECLS["l2_attn"] = decl_l2_attn
STAGERS["l2_attn"] = stage_l2_attn




def emit_l0_attn(ctx, xT_in, xT_out, wd, C, scr):
    g = ctx.fresh("l0a")
    scr6 = ctx.scr6
    xln = ctx.xln
    wtiles = ctx.wtiles
    ln_w = emit_layernorm(ctx, xln, C, xT_dram=xT_in, scr6=scr6)

    Wi, gW = load_weight_bf16(ctx, wd["Wi"], D, D, out=wtiles, waits=ln_w)
    di, _2 = emit_proj_tm(ctx, xln, Wi, scr["itm"], ln_w + [("gs", gW)], C)
    Wp, gW = load_weight_bf16(ctx, wd["Wp"], D, D, out=wtiles, waits=[("dv", di)])
    dp, _2 = emit_proj_tm(ctx, xln, Wp, scr["ptm"], [("dv", di), ("gs", gW)], C)
    pd = [("dv", dp), ("pfs0", ctx.cnt("pfs0")), ("pfs1", ctx.cnt("pfs1"))]

    cs = ctx.carve("f32", 1024)
    rin = ctx.carve("bf2", 1024)
    rop = ctx.carve("f32", 1024)
    nr2 = ctx.carve("f32", 1024)
    psT_ = [ctx.pool["PC"], ctx.pool["PD"]]
    last = None
    for which, src_d, fm_d, nrow in ((0, scr["ptm"], scr["pT"], 0), (1, scr["itm"], scr["iT"], T)):
        for i in range(NT):
            w = (pd if last is None else [("dv", last)])
            lc1 = ctx.dma("sync", cs[:, 0:512], wd["cos"][i * 128:(i + 1) * 128, :],
                          waits=w, sem_name="kt0")
            lc2 = ctx.dma("sync", cs[:, 512:1024], wd["sin"][i * 128:(i + 1) * 128, :],
                          sem_name="kt0")
            lx = ctx.dma("sync", rin[:, :], src_d[i * 128:(i + 1) * 128, :],
                         waits=w, sem_name="kt1")
            dd = ctx.step("vector", lambda v: v.tensor_tensor(
                rop[:, 0:512], rin[:, 0:512], cs[:, 0:512], ALU.mult),
                waits=[("kt0", lc2), ("kt1", lx)])
            dd = ctx.step("vector", lambda v: v.tensor_tensor(
                nr2[:, 0:512], rin[:, 512:1024], cs[:, 512:1024], ALU.mult),
                waits=[("dv", dd)])
            dd = ctx.step("vector", lambda v: v.tensor_tensor(
                rop[:, 0:512], rop[:, 0:512], nr2[:, 0:512], ALU.subtract),
                waits=[("dv", dd)])
            dd = ctx.step("vector", lambda v: v.tensor_tensor(
                rop[:, 512:1024], rin[:, 0:512], cs[:, 512:1024], ALU.mult),
                waits=[("dv", dd)])
            dd = ctx.step("vector", lambda v: v.tensor_tensor(
                nr2[:, 512:1024], rin[:, 512:1024], cs[:, 0:512], ALU.mult),
                waits=[("dv", dd)])
            dd = ctx.step("vector", lambda v: v.tensor_tensor(
                rop[:, 512:1024], rop[:, 512:1024], nr2[:, 512:1024], ALU.add),
                waits=[("dv", dd)])
            dd = ctx.step("vector", lambda v: v.tensor_tensor(
                nr2[:, :], rop[:, :], rop[:, :], ALU.mult), waits=[("dv", dd)])
            dd = ctx.step("vector", lambda v: v.tensor_reduce(
                scr6[0][:, 0:1], nr2[:, :], axis=AX.X, op=ALU.add), waits=[("dv", dd)])
            a = ctx.step("scalar", lambda sc: sc.activation(
                scr6[0][:, 1:2], scr6[0][:, 0:1], AF.Sqrt), waits=[("dv", dd)])
            dd = ctx.step("vector", lambda v: v.tensor_scalar(
                scr6[0][:, 1:2], scr6[0][:, 1:2], 1e-12, None, op0=ALU.max), waits=[("ac", a)])
            dd = ctx.step("vector", lambda v: v.reciprocal(
                scr6[0][:, 2:3], scr6[0][:, 1:2]), waits=[("dv", dd)])
            dd = ctx.step("vector", lambda v: v.tensor_scalar(
                rin[:, :], rop[:, :], scr6[0][:, 2:3], None, op0=ALU.mult), waits=[("dv", dd)])
            ctx.dma("sync", scr["nrm"][nrow + i * 128:nrow + (i + 1) * 128, :], rin[:, :],
                    waits=[("dv", dd)], sem_name="sy0")
            for c8 in range(8):
                p_ = ctx.step("tensor", lambda t, c8=c8: t.transpose(
                    psT_[c8 % 2][:, 0:128], rop[:, c8 * 128:(c8 + 1) * 128], C["ident"][:, :]),
                    waits=[("dv", dd)] if c8 < 2 else [("dv", last)])
                last = ctx.step("vector", lambda v, c8=c8: v.tensor_copy(
                    ctx.evbf[c8 % 2][:, 0:128], psT_[c8 % 2][:, 0:128]),
                    waits=[("pe", p_), (f"sy{c8 % 2}", ctx.cnt(f"sy{c8 % 2}"))])
                ctx.dma("sync", fm_d[c8 * 128:(c8 + 1) * 128, i * 128:(i + 1) * 128],
                        ctx.evbf[c8 % 2][:, 0:128], waits=[("dv", last)],
                        sem_name=f"sy{c8 % 2}")
    rope_done = [("dv", last), ("sy0", ctx.cnt("sy0")), ("sy1", ctx.cnt("sy1"))]

    lw1 = ctx.dma("gpsimd", wtiles[0][0:16, 0:64], wd["W1"][0:16, :], waits=rope_done, sem_name="gla")
    lw1b = ctx.dma("gpsimd", wtiles[0][0:1, 64:128], wd["W1"][16:17, :], sem_name="gla")
    lw2 = ctx.dma("gpsimd", wtiles[0][0:64, 128:192], wd["W2"], sem_name="gla")
    lwc = ctx.dma("gpsimd", wtiles[0][0:64, 192:194], wd["Wc"], sem_name="gla")
    lwm = ctx.dma("gpsimd", wtiles[0][0:64, 194:195], wd["Wm"], sem_name="gla")
    lpal = ctx.dma("gpsimd", wtiles[1][:, 0:1024], wd["palT"][0:128, :], sem_name="gla")
    lpal = ctx.dma("gpsimd", wtiles[2][:, 0:1024], wd["palT"][128:256, :], sem_name="gla")
    lfm = ctx.dma("gpsimd", wtiles[0][:, 256:272], wd["F"], sem_name="gla")
    lbd = ctx.dma("gpsimd", wtiles[0][:, 272:400], wd["BD"], sem_name="gla")
    bcol, bld = load_cols_f32(ctx, wd["bvec"], 1)
    zc = ctx.sb(g + "zc", [2, 1], F32)
    zm = ctx.sb(g + "zm1", [1, 1], F32)
    lz1 = ctx.dma("sync", zc[:, :], wd["bc"].rearrange("(a o) -> a o", o=1), sem_name="cst")
    lz2 = ctx.dma("sync", zm[:, :], wd["bm"].rearrange("(a o) -> a o", o=1), sem_name="cst")
    wload = [("gla", lbd), ("cst", ctx.cnt("cst")), ("dv", bld)]

    ctx.reset_arenas()
    psel = ctx.carve("bf", 8192)
    pselT = ctx.evbf[0]
    relT = ctx.carve("bf", 2048)
    h1b = ctx.carve("bf", 1024)
    h2b = ctx.evbf[1]
    STb = ctx.carve("bf", 128)
    ptst = ctx.carve("bf", 4096)
    vm0 = ctx.carve("bf", 512)
    sc_buf = ctx._ar["f32b"]
    m16 = ctx.carve("f32", 16)
    tiota = ctx.carve("f32", 16)
    gf = ctx.carve("f32", 1024)
    dif = ctx.carve("f32", 512)
    i16 = ctx.sb(g + "i16", [128, 16], U32)
    mk = ctx.step("gpsimd", lambda gp: gp.iota(
        dif, pattern=[[1, 512]], base=0, channel_multiplier=-1,
        allow_small_or_imprecise_dtypes=True), waits=wload, inc_sem="gsi", inc=1)
    mk = ctx.step("gpsimd", lambda gp: gp.iota(
        tiota[:, 0:1], pattern=[[1, 1]], base=0, channel_multiplier=1,
        allow_small_or_imprecise_dtypes=True), waits=[("gsi", mk)], inc_sem="gsi", inc=1)
    dmk = ctx.step("vector", lambda v: v.tensor_scalar(
        vm0, dif, -0.5, NEG, op0=ALU.is_lt, op1=ALU.mult), waits=[("gsi", mk)])
    tail = ("dv", dmk)
    psS = [ctx.pool["PC"], ctx.pool["PD"]]
    psT2 = [ctx.pool["PC"], ctx.pool["PD"]]
    iT_v = scr["iT"].rearrange("(c p) t -> p c t", p=128)
    pT_v = scr["pT"].rearrange("(c p) t -> p c t", p=128)
    qkt = ctx.carve("bf2", 1024)

    for qi in range(NT):
        nb = qi // 4 + 1
        lq = ctx.dma("sync", qkt.rearrange("p (c f) -> p c f", c=8),
                     iT_v[:, :, qi * 128:(qi + 1) * 128], waits=[tail], sem_name="qg")
        ev = {}
        lkprev = None
        for b in range(nb):
            lk = ctx.dma("sync", ptst.rearrange("p (c f) -> p c f", c=8),
                         pT_v[:, :, b * 512:(b + 1) * 512],
                         waits=([tail] if b == 0 else [("pe", pmm)]), sem_name="kt0")

            def mm(t, qi=qi, b=b):
                for k in range(8):
                    h_ = t.matmul(psS[b % 2][:, :], qkt[:, k * 128:(k + 1) * 128],
                                  ptst[:, k * 512:(k + 1) * 512], start=(k == 0), stop=(k == 7))
                return h_
            pmm = ctx.step("tensor", mm, waits=[("kt0", lk), ("qg", lq)] +
                           ([("dv", ev[b - 2])] if b >= 2 else [tail]))
            sh = qi * 128 - b * 512
            dd = None
            for st in range(4):
                if 128 * st > sh:
                    dd = ctx.step("vector", lambda v, b=b, st=st: v.memset(
                        sc_buf[:, b * 512 + st * 128: b * 512 + (st + 1) * 128], NEG),
                        waits=[("pe", pmm)] if dd is None else [("dv", dd)])
                elif 128 * st == sh:
                    dd = ctx.step("vector", lambda v, b=b, st=st: v.scalar_tensor_tensor(
                        sc_buf[:, b * 512 + st * 128: b * 512 + (st + 1) * 128],
                        psS[b % 2][:, st * 128:(st + 1) * 128], 1.0 / 32.0, vm0[:, 0:128],
                        op0=ALU.mult, op1=ALU.add), waits=([("pe", pmm)] if dd is None else [("dv", dd)]))
                else:
                    dd = ctx.step("vector", lambda v, b=b, st=st: v.tensor_scalar(
                        sc_buf[:, b * 512 + st * 128: b * 512 + (st + 1) * 128],
                        psS[b % 2][:, st * 128:(st + 1) * 128], 1.0 / 32.0, None,
                        op0=ALU.mult), waits=[("pe", pmm)] if dd is None else [("dv", dd)])
            ev[b] = dd
        L = nb * 512
        d = ev[nb - 1]
        d = ctx.step("vector", lambda v, L=L: v.max(out=m16[:, 0:8], in_=sc_buf[:, 0:L]), waits=[("dv", d)])
        d = ctx.step("vector", lambda v, L=L: v.max_index(out=i16[:, 0:8], in_max=m16[:, 0:8],
                     in_values=sc_buf[:, 0:L]), waits=[("dv", d)])
        d = ctx.step("vector", lambda v, L=L: v.match_replace(out=sc_buf[:, 0:L],
                     in_to_replace=m16[:, 0:8], in_values=sc_buf[:, 0:L], imm_value=NEG),
                     waits=[("dv", d)])
        d = ctx.step("vector", lambda v, L=L: v.max(out=m16[:, 8:16], in_=sc_buf[:, 0:L]), waits=[("dv", d)])
        d = ctx.step("vector", lambda v, L=L: v.max_index(out=i16[:, 8:16], in_max=m16[:, 8:16],
                     in_values=sc_buf[:, 0:L]), waits=[("dv", d)])
        d = ctx.step("vector", lambda v: v.tensor_copy(gf[:, 0:16], i16[:, 0:16]), waits=[("dv", d)])
        d = ctx.step("vector", lambda v: v.tensor_scalar(gf[:, 16:32], m16[:, 0:16],
                     -1e29, None, op0=ALU.is_gt), waits=[("dv", d)])
        d = ctx.step("vector", lambda v: v.memset(gf[:, 31:32], 0.0), waits=[("dv", d)])
        d = ctx.step("vector", lambda v: v.tensor_copy(gf[:, 32:48], gf[:, 0:16]), waits=[("dv", d)])
        d = ctx.step("vector", lambda v: v.tensor_scalar(gf[:, 32:48], gf[:, 32:48],
                     tiota[:, 0:1], None, op0=ALU.subtract), waits=[("dv", d)])
        d = ctx.step("vector", lambda v: v.tensor_scalar(gf[:, 32:48], gf[:, 32:48],
                     float(qi * 128), None, op0=ALU.subtract), waits=[("dv", d)])
        d = ctx.step("vector", lambda v: v.tensor_scalar(gf[:, 32:48], gf[:, 32:48],
                     -1.0 / T, None, op0=ALU.mult), waits=[("dv", d)])
        d = ctx.step("vector", lambda v: v.tensor_scalar(gf[:, 32:48], gf[:, 32:48],
                     0.0, None, op0=ALU.max), waits=[("dv", d)])
        d = ctx.step("vector", lambda v: v.tensor_tensor(gf[:, 32:48], gf[:, 32:48],
                     gf[:, 16:32], ALU.mult), waits=[("dv", d)])
        d = ctx.step("vector", lambda v: v.tensor_copy(gf[:, 48:64], gf[:, 0:16]),
                     waits=[("dv", d)])
        d = ctx.step("vector", lambda v: v.tensor_scalar(gf[:, 63:64], tiota[:, 0:1],
                     float(T + qi * 128), None, op0=ALU.add), waits=[("dv", d)])
        d = ctx.step("vector", lambda v: v.tensor_copy(i16[:, 0:16], gf[:, 48:64]),
                     waits=[("dv", d)])
        w1 = ctx.dma("sync", scr["idx"][qi * 2048:(qi + 1) * 2048].rearrange("(p k) -> p k", p=128),
                     i16[:, 0:16], waits=[("dv", d)], sem_name="sy0")
        w2 = ctx.dma("sync", scr["dk"][0, qi * 2048:(qi + 1) * 2048].rearrange("(p k) -> p k", p=128),
                     gf[:, 32:48], waits=[("dv", d)], sem_name="sy0")
        w3 = ctx.dma("sync", scr["dk"][1, qi * 2048:(qi + 1) * 2048].rearrange("(p k) -> p k", p=128),
                     gf[:, 16:32], waits=[("dv", d)], sem_name="sy0")
        d = ctx.step("vector", lambda v: v.memset(gf[:, 31:32], 1.0),
                     waits=[("dv", d), ("sy0", w3)])
        w4 = ctx.dma("sync", scr["dk"][2, qi * 2048:(qi + 1) * 2048].rearrange("(p k) -> p k", p=128),
                     gf[:, 16:32], waits=[("dv", d)], sem_name="sy0")
        tail = ("dv", d)

        for hf in range(2):
            slot0 = qi * 2048 + hf * 1024
            dz = ctx.step("vector", lambda v: v.memset(psel[:, :], 0.0), waits=[tail, ("sy0", w4)])
            for gg in range(8):
                lix = ctx.dma("sync", i16[:, gg:gg + 1],
                              scr["idx"][slot0 + gg * 128: slot0 + (gg + 1) * 128].rearrange("(p o) -> p o", o=1),
                              waits=([("dv", dz)] if gg == 0 else []), sem_name="kt1")
            for gg in range(8):
                lix = ctx.dma("sync", m16[:, gg:gg + 1],
                              scr["dk"][2, slot0 + gg * 128: slot0 + (gg + 1) * 128].rearrange("(p o) -> p o", o=1),
                              sem_name="kt1")
            gdma = None
            for gg in range(8):
                gdma = ctx.dma("gpsimd", psel[:, gg * 1024:(gg + 1) * 1024], scr["nrm"],
                               waits=[("kt1", lix)], sem_name="gla",
                               in_offset=IOX(i16[:, gg:gg + 1], 0))
            dmul = ctx.step("vector", lambda v: v.tensor_tensor(
                psel.rearrange("p (g f) -> p g f", g=8),
                psel.rearrange("p (g f) -> p g f", g=8),
                m16[:, 0:8].rearrange("p (g o) -> p g o", o=1).to_broadcast([128, 8, 1024]),
                ALU.mult), waits=[("gla", gdma), ("kt1", lix)])
            tail2g = None
            for gg in range(8):
                tp = {}
                for c8 in range(8):
                    p_ = ctx.step("tensor", lambda t, gg=gg, c8=c8: t.matmul(
                        psT2[c8 % 2][:, 0:128],
                        psel[:, gg * 1024 + c8 * 128: gg * 1024 + (c8 + 1) * 128],
                        C["identb"][:, :], start=True, stop=True),
                        waits=([("dv", dmul)] if (gg == 0 and c8 < 2) else
                               ([("dv", tp[c8 - 2])] if c8 >= 2 else [("dv", tail2g)])))
                    tp[c8] = ctx.step("vector", lambda v, c8=c8: v.tensor_copy(
                        pselT[:, c8 * 128:(c8 + 1) * 128], psT2[c8 % 2][:, 0:128]),
                        waits=[("pe", p_)])
                def mmg(t, gg=gg):
                    for c8 in range(8):
                        h_ = t.matmul(ctx.pool["PA"][:, 0:128], pselT[:, c8 * 128:(c8 + 1) * 128],
                                      pselT[:, c8 * 128:(c8 + 1) * 128], start=(c8 == 0), stop=(c8 == 7))
                    return h_
                pg = ctx.step("tensor", mmg, waits=[("dv", tp[7])])
                dgc = ctx.step("vector", lambda v: v.tensor_tensor(
                    scr6[1][:, 0:128], ctx.pool["PA"][:, 0:128], wtiles[0][:, 272:400], ALU.mult),
                    waits=[("pe", pg)])
                dgc = ctx.step("vector", lambda v: v.tensor_scalar(
                    scr6[1][:, 0:128], scr6[1][:, 0:128], 1.0, -1.0, op0=ALU.min, op1=ALU.max),
                    waits=[("dv", dgc)])
                dgc = ctx.step("vector", lambda v: v.tensor_copy(
                    pselT[:, 0:128], scr6[1][:, 0:128]), waits=[("dv", dgc)])
                pe2 = ctx.step("tensor", lambda t, gg=gg: t.matmul(
                    ctx.pool["PB"][0:16, 0:128], wtiles[0][:, 256:272], pselT[:, 0:128],
                    start=True, stop=True), waits=[("dv", dgc)])
                tail2g = ctx.step("vector", lambda v, gg=gg, hf=hf: v.tensor_copy(
                    relT[0:16, hf * 1024 + gg * 128: hf * 1024 + (gg + 1) * 128],
                    ctx.pool["PB"][0:16, 0:128]), waits=[("pe", pe2)])
            ldel = ctx.dma("gpsimd", ctx.rowbf[0][:, 0:1024],
                           scr["dk"][0, slot0:slot0 + 1024].rearrange("(o f) -> o f", o=1),
                           waits=[("dv", tail2g)], sem_name="gla")

            def mh1(t, hf=hf):
                for half in range(2):
                    hs = slice(half * 512, half * 512 + 512)
                    t.matmul(ctx.pool["PA"][0:64, hs], wtiles[0][0:16, 0:64],
                             relT[0:16, hf * 1024 + half * 512: hf * 1024 + half * 512 + 512],
                             start=True, stop=False)
                    h_ = t.matmul(ctx.pool["PA"][0:64, hs], wtiles[0][0:1, 64:128],
                                  ctx.rowbf[0][:, hs], start=False, stop=True)
                return h_
            p3 = ctx.step("tensor", mh1, waits=[("gla", ldel), ("dv", tail2g)])
            a3 = ctx.step("scalar", lambda sc: sc.activation(
                h1b[0:64, :], ctx.pool["PA"][0:64, 0:1024], AF.Gelu, bias=bcol[0:64, 0:1]),
                waits=[("pe", p3)])
            def mh2(t):
                for half in range(2):
                    hs = slice(half * 512, half * 512 + 512)
                    h_ = t.matmul(ctx.pool["PA"][0:64, hs], wtiles[0][0:64, 128:192],
                                  h1b[0:64, hs], start=True, stop=True)
                return h_
            p4 = ctx.step("tensor", mh2, waits=[("ac", a3)])
            a4 = ctx.step("scalar", lambda sc: sc.activation(
                h2b[0:64, 0:1024], ctx.pool["PA"][0:64, 0:1024], AF.Gelu, bias=bcol[64:128, 0:1]),
                waits=[("pe", p4), ("sy1", ctx.cnt("sy1"))])
            def mh5(t):
                for half in range(2):
                    hs = slice(half * 512, half * 512 + 512)
                    h_ = t.matmul(ctx.pool["PA"][0:2, hs], wtiles[0][0:64, 192:194],
                                  h2b[0:64, hs], start=True, stop=True)
                return h_
            p5 = ctx.step("tensor", mh5, waits=[("ac", a4)])

            def mh6(t):
                for half in range(2):
                    hs = slice(half * 512, half * 512 + 512)
                    h_ = t.matmul(ctx.pool["PR"][0:1, hs], wtiles[0][0:64, 194:195],
                                  h2b[0:64, hs], start=True, stop=True)
                return h_
            p6 = ctx.step("tensor", mh6, waits=[("ac", a4)])
            a5 = ctx.step("scalar", lambda sc: sc.activation(
                scr6[2][0:2, 0:512], ctx.pool["PA"][0:2, 0:512], AF.Tanh, bias=zc[0:2, 0:1]),
                waits=[("pe", p5)])
            a5 = ctx.step("scalar", lambda sc: sc.activation(
                scr6[3][0:2, 0:512], ctx.pool["PA"][0:2, 512:1024], AF.Tanh, bias=zc[0:2, 0:1]),
                waits=[("ac", a5)])
            d6 = ctx.step("vector", lambda v: v.tensor_scalar(
                gf[0:1, 0:1024], ctx.pool["PR"][0:1, 0:1024], zm[0:1, 0:1], None,
                op0=ALU.add), waits=[("pe", p6), ("ac", a5)])
            wz = ctx.dma("sync", scr["zm"][0:2, 0:512], scr6[2][0:2, 0:512],
                         waits=[("ac", a5)], sem_name="sy1")
            wz = ctx.dma("sync", scr["zm"][0:2, 512:1024], scr6[3][0:2, 0:512],
                         waits=[("ac", a5)], sem_name="sy1")
            wz = ctx.dma("sync", scr["zm"][2:3, 0:1024], gf[0:1, 0:1024],
                         waits=[("dv", d6)], sem_name="sy1")
            lz = ctx.dma("sync", gf[0:64, 0:16],
                         scr["zm"][0, :].rearrange("(p k) -> p k", p=64),
                         waits=[("sy1", wz)], sem_name="kt0")
            lz = ctx.dma("sync", gf[0:64, 16:32],
                         scr["zm"][1, :].rearrange("(p k) -> p k", p=64), sem_name="kt0")
            lz = ctx.dma("sync", gf[0:64, 32:48],
                         scr["zm"][2, :].rearrange("(p k) -> p k", p=64), sem_name="kt0")
            lkp = ctx.dma("sync", gf[0:64, 48:64],
                          scr["dk"][1, slot0:slot0 + 1024].rearrange("(p k) -> p k", p=64),
                          sem_name="kt0")
            d7 = ctx.step("vector", lambda v: v.tensor_tensor(
                scr6[0][0:64, 0:16], gf[0:64, 32:48], gf[0:64, 48:64], ALU.mult),
                waits=[("kt0", ctx.cnt("kt0"))])
            d7 = ctx.step("vector", lambda v: v.tensor_scalar(
                scr6[0][0:64, 16:32], gf[0:64, 48:64], -1.0, -200.0, op0=ALU.subtract, op1=ALU.mult),
                waits=[("dv", d7)])
            d7 = ctx.step("vector", lambda v: v.tensor_tensor(
                scr6[0][0:64, 0:16], scr6[0][0:64, 0:16], scr6[0][0:64, 16:32], ALU.subtract),
                waits=[("dv", d7)])
            d7 = ctx.step("vector", lambda v: v.tensor_reduce(
                scr6[0][0:64, 32:33], scr6[0][0:64, 0:16], axis=AX.X, op=ALU.max), waits=[("dv", d7)])
            d7 = ctx.step("vector", lambda v: v.tensor_scalar(
                scr6[0][0:64, 0:16], scr6[0][0:64, 0:16], scr6[0][0:64, 32:33], None,
                op0=ALU.subtract), waits=[("dv", d7)])
            a7 = ctx.step("scalar", lambda sc: sc.activation(
                scr6[0][0:64, 0:16], scr6[0][0:64, 0:16], AF.Exp,
                accum_out=scr6[0][0:64, 33:34]), waits=[("dv", d7)])
            d7 = ctx.step("vector", lambda v: v.reciprocal(
                scr6[0][0:64, 34:35], scr6[0][0:64, 33:34]), waits=[("ac", a7)])
            d7 = ctx.step("vector", lambda v: v.tensor_scalar(
                scr6[0][0:64, 0:16], scr6[0][0:64, 0:16], scr6[0][0:64, 34:35], None,
                op0=ALU.mult), waits=[("dv", d7)])
            d8 = ctx.step("vector", lambda v: v.tensor_scalar(
                gf[0:64, 0:32], gf[0:64, 0:32], 1.0, 7.5, op0=ALU.add, op1=ALU.mult),
                waits=[("dv", d7)])
            d8 = ctx.step("vector", lambda v: v.tensor_scalar(
                gf[0:64, 0:32], gf[0:64, 0:32], 0.0, 15.0, op0=ALU.max, op1=ALU.min),
                waits=[("dv", d8)])
            d8 = ctx.step("vector", lambda v: v.tensor_tensor(
                scr6[1][0:64, 0:256].rearrange("p (k y) -> p k y", k=16),
                gf[0:64, 16:32].rearrange("p (k o) -> p k o", o=1).to_broadcast([64, 16, 16]),
                C["iota16"][0:64, :].rearrange("p (o y) -> p o y", o=1).to_broadcast([64, 16, 16]),
                ALU.subtract), waits=[("dv", d8)])
            ab8 = ctx.step("scalar", lambda sc: sc.activation(
                scr6[1][0:64, 0:256], scr6[1][0:64, 0:256], AF.Abs), waits=[("dv", d8)])
            a8 = ctx.step("scalar", lambda sc: sc.activation(
                scr6[1][0:64, 0:256], scr6[1][0:64, 0:256], AF.Relu, scale=-1.0,
                bias=C["ones"][0:64, 0:1]), waits=[("ac", ab8)])
            d8 = ctx.step("vector", lambda v: v.tensor_tensor(
                scr6[1][0:64, 0:256].rearrange("p (k y) -> p k y", k=16),
                scr6[1][0:64, 0:256].rearrange("p (k y) -> p k y", k=16),
                scr6[0][0:64, 0:16].rearrange("p (k o) -> p k o", o=1).to_broadcast([64, 16, 16]),
                ALU.mult), waits=[("ac", a8)])
            d9 = ctx.step("vector", lambda v: v.tensor_tensor(
                scr6[2][0:64, 0:256].rearrange("p (k x) -> p k x", k=16),
                gf[0:64, 0:16].rearrange("p (k o) -> p k o", o=1).to_broadcast([64, 16, 16]),
                C["iota16"][0:64, :].rearrange("p (o y) -> p o y", o=1).to_broadcast([64, 16, 16]),
                ALU.subtract), waits=[("dv", d8)])
            ab9 = ctx.step("scalar", lambda sc: sc.activation(
                scr6[2][0:64, 0:256], scr6[2][0:64, 0:256], AF.Abs), waits=[("dv", d9)])
            a9 = ctx.step("scalar", lambda sc: sc.activation(
                scr6[2][0:64, 0:256], scr6[2][0:64, 0:256], AF.Relu, scale=-1.0,
                bias=C["ones"][0:64, 0:1]), waits=[("ac", ab9)])
            dS = ctx.step("vector", lambda v: v.tensor_tensor(
                sc_buf[0:64, 0:4096].rearrange("p (y x k) -> p y x k", y=16, x=16),
                scr6[1][0:64, 0:256].rearrange("p (k y o) -> p y o k", k=16, o=1).to_broadcast([64, 16, 16, 16]),
                scr6[2][0:64, 0:256].rearrange("p (k x o) -> p o x k", k=16, o=1).to_broadcast([64, 16, 16, 16]),
                ALU.mult), waits=[("ac", a9)])
            dS = ctx.step("vector", lambda v: v.tensor_reduce(
                scr6[5][0:64, 0:256], sc_buf[0:64, 0:4096].rearrange("p (y x k) -> p y x k", y=16, x=16),
                axis=AX.X, op=ALU.add), waits=[("dv", dS)])
            dS = ctx.step("vector", lambda v: v.tensor_copy(
                h1b[0:64, 0:256], scr6[5][0:64, 0:256]), waits=[("dv", dS)])
            tS = {}
            for cc in range(2):
                pS = ctx.step("tensor", lambda t, cc=cc: t.matmul(
                    psT2[cc][:, 0:64], h1b[0:64, cc * 128:(cc + 1) * 128], C["identb"][0:64, 0:64],
                    start=True, stop=True), waits=[("dv", dS)])
                tS[cc] = ctx.step("vector", lambda v, cc=cc: v.tensor_copy(
                    STb[:, cc * 64:(cc + 1) * 64], psT2[cc][:, 0:64]), waits=[("pe", pS)])
            vdone = None
            for dc in range(8):
                pV = ctx.step("tensor", lambda t, dc=dc: (
                    t.matmul(ctx.pool["PA"][:, 0:64], wtiles[1][:, dc * 128:(dc + 1) * 128],
                             STb[:, 0:64], start=True, stop=False),
                    t.matmul(ctx.pool["PA"][:, 0:64], wtiles[2][:, dc * 128:(dc + 1) * 128],
                             STb[:, 64:128], start=False, stop=True))[-1],
                    waits=[("dv", tS[1])] if dc == 0 else [("dv", vdone)])
                vdone = ctx.step("vector", lambda v, dc=dc: v.tensor_copy(
                    ctx.evbf[1][:, 0:64] if False else h2b[:, 0:64], ctx.pool["PA"][:, 0:64]),
                    waits=[("pe", pV), ("sy1", ctx.cnt("sy1"))])
                ctx.dma("sync", scr["aoutT"][dc * 128:(dc + 1) * 128,
                        qi * 128 + hf * 64: qi * 128 + hf * 64 + 64],
                        h2b[:, 0:64], waits=[("dv", vdone)], sem_name="sy1")
            tail = ("dv", vdone)
    att_done = [tail, ("sy1", ctx.cnt("sy1")), ("sy0", ctx.cnt("sy0"))]
    lds = None
    for c in range(8):
        lds = ctx.dma("gpsimd", xln[c][:, :], scr["aoutT"][c * 128:(c + 1) * 128, :],
                      waits=(att_done if c == 0 else ()), sem_name="gla")
    return emit_out_proj(ctx, xln, wd["Wo"], xT_out, [("gla", lds)], res_dram=xT_in)


def decl_l0_attn(nc):
    wd = {
        "Wi": nc.dram_tensor("l0a_Wi", [D, D], F32, kind="ExternalInput").ap(),
        "Wp": nc.dram_tensor("l0a_Wp", [D, D], F32, kind="ExternalInput").ap(),
        "W1": nc.dram_tensor("l0a_W1", [17, 64], F32, kind="ExternalInput").ap(),
        "W2": nc.dram_tensor("l0a_W2", [64, 64], F32, kind="ExternalInput").ap(),
        "Wc": nc.dram_tensor("l0a_Wc", [64, 2], F32, kind="ExternalInput").ap(),
        "Wm": nc.dram_tensor("l0a_Wm", [64, 1], F32, kind="ExternalInput").ap(),
        "Wo": nc.dram_tensor("l0a_Wo", [D, D], F32, kind="ExternalInput").ap(),
        "bvec": nc.dram_tensor("l0a_bvec", [512], F32, kind="ExternalInput").ap(),
        "bc": nc.dram_tensor("l0a_bc", [2], F32, kind="ExternalInput").ap(),
        "bm": nc.dram_tensor("l0a_bm", [1], F32, kind="ExternalInput").ap(),
        "palT": nc.dram_tensor("l0a_palT", [256, D], F32, kind="ExternalInput").ap(),
        "F": nc.dram_tensor("l0a_F", [128, 16], F32, kind="ExternalInput").ap(),
        "BD": nc.dram_tensor("l0a_BD", [128, 128], F32, kind="ExternalInput").ap(),
        "cos": nc.dram_tensor("l0a_cos", [T, 512], F32, kind="ExternalInput").ap(),
        "sin": nc.dram_tensor("l0a_sin", [T, 512], F32, kind="ExternalInput").ap(),
    }
    scr = {
        "itm": nc.dram_tensor("l0a_itm", [T, D], BF16, kind="Internal").ap(),
        "ptm": nc.dram_tensor("l0a_ptm", [T, D], BF16, kind="Internal").ap(),
        "iT": nc.dram_tensor("l0a_iT", [D, T], BF16, kind="Internal").ap(),
        "pT": nc.dram_tensor("l0a_pT", [D, T], BF16, kind="Internal").ap(),
        "nrm": nc.dram_tensor("l0a_nrm", [2 * T, D], BF16, kind="Internal").ap(),
        "idx": nc.dram_tensor("l0a_idx", [T * 16], U32, kind="Internal").ap(),
        "dk": nc.dram_tensor("l0a_dk", [3, T * 16], F32, kind="Internal").ap(),
        "zm": nc.dram_tensor("l0a_zm", [3, 1024], F32, kind="Internal").ap(),
        "aoutT": nc.dram_tensor("l0a_aoutT", [D, T], BF16, kind="Internal").ap(),
    }
    return wd, scr


def stage_l0_attn(inp):
    f = np.ascontiguousarray
    half = D // 2
    inv = 1.0 / 10000.0 ** (np.arange(half, dtype=np.float64) / half)
    ang = np.arange(T, dtype=np.float64)[:, None] * inv[None, :]
    bvec = np.zeros(512, np.float32)
    bvec[0:64] = inp["L0_attn_b1"].astype(np.float32)
    bvec[64:128] = inp["L0_attn_b2"].astype(np.float32)
    F = np.zeros((128, 16), np.float32)
    for i in range(8):
        for jj in range(16):
            F[16 * i + jj, jj] = 1.0
    BD = np.zeros((128, 128), np.float32)
    for i in range(8):
        BD[16 * i:16 * (i + 1), 16 * i:16 * (i + 1)] = 1.0
    return {
        "l0a_Wi": f(inp["L0_attn_Wi"].astype(np.float32)),
        "l0a_Wp": f(inp["L0_attn_Wp"].astype(np.float32)),
        "l0a_W1": f(inp["L0_attn_W1"].astype(np.float32)),
        "l0a_W2": f(inp["L0_attn_W2"].astype(np.float32)),
        "l0a_Wc": f(inp["L0_attn_Wc"].astype(np.float32)),
        "l0a_Wm": f(inp["L0_attn_Wm"].astype(np.float32)),
        "l0a_Wo": f(inp["L0_attn_Wo"][0].astype(np.float32)),
        "l0a_bvec": f(bvec),
        "l0a_bc": f(inp["L0_attn_bc"].astype(np.float32)),
        "l0a_bm": f(inp["L0_attn_bm"].astype(np.float32)),
        "l0a_palT": f(inp["L0_attn_palette"].astype(np.float32).reshape(D, 256).T),
        "l0a_F": F,
        "l0a_BD": BD,
        "l0a_cos": f(np.cos(ang).astype(np.float32)),
        "l0a_sin": f(np.sin(ang).astype(np.float32)),
    }


EMITTERS["l0_attn"] = emit_l0_attn
DECLS["l0_attn"] = decl_l0_attn
STAGERS["l0_attn"] = stage_l0_attn



CAP = 1024


def emit_l2f_a(ctx, xT_in, xT_out, wd, C, scr, phases=9):
    g = ctx.fresh("l2fa")
    scr6 = ctx.scr6
    xln = ctx.xln
    ln_w = emit_layernorm(ctx, xln, C, xT_dram=xT_in, scr6=scr6)

    psT = [ctx.pool["PC"], ctx.pool["PD"]]
    last = None
    for i in range(NT):
        for c8 in range(8):
            w = (list(ln_w) if last is None else [("dv", last)])
            p_ = ctx.step("tensor", lambda t, i=i, c8=c8: t.matmul(
                psT[c8 % 2][:, 0:128], xln[c8][:, i * 128:(i + 1) * 128],
                C["identb"][:, :], start=True, stop=True), waits=w)
            last = ctx.step("vector", lambda v, i=i, c8=c8: v.tensor_copy(
                ctx.evbf[i % 2][:, c8 * 128:(c8 + 1) * 128], psT[c8 % 2][:, 0:128]),
                waits=[("pe", p_)] +
                ([(f"sy{i % 2}", ctx.cnt(f"sy{i % 2}"))] if c8 == 0 else []))
        ctx.dma("sync", scr["xlnr"][i * 128:(i + 1) * 128, :], ctx.evbf[i % 2][:, :],
                waits=[("dv", last)], sem_name=f"sy{i % 2}")
    tm_done = [("dv", last), ("sy0", ctx.cnt("sy0")), ("sy1", ctx.cnt("sy1"))]
    if phases <= 1:
        return tm_done

    gwt = ctx.carve("bf2", 512)
    lg = ctx.dma("gpsimd", gwt, wd["gwt"], waits=ln_w, sem_name="gla")
    ebrow = ctx.rowbf2[:, 0:64]
    le = ctx.dma("sync", ebrow, wd["ebias"].rearrange("(o f) -> o f", o=1), sem_name="cst")
    ebcol = ctx.sb(g + "ebc", [64, 1], F32)
    le2 = ctx.dma("sync", ebcol[:, :], wd["ebias"].rearrange("(p o) -> p o", o=1), sem_name="cst")
    pb_ = ctx.step("tensor", lambda t: t.matmul(ctx.pool["PB"][:, 0:64], C["ones"][0:1, :],
                   ebrow, start=True, stop=True), waits=[("cst", le2)] + tm_done)
    ebt = ctx.carve("f32", 64)
    deb = ctx.step("vector", lambda v: v.tensor_copy(ebt, ctx.pool["PB"][:, 0:64]),
                   waits=[("pe", pb_)])

    sfb = ctx._ar["f32b"]
    aprev = None
    dbias = None
    for nb in range(8):
        s = slice(nb * 512, nb * 512 + 512)

        def mm(t, s=s):
            for c in range(8):
                h = t.matmul(ctx.pool["PC"][0:64, :], gwt[:, c * 64:(c + 1) * 64],
                             xln[c][:, s], start=(c == 0), stop=(c == 7))
            return h
        p = ctx.step("tensor", mm,
                     waits=(tm_done + [("gla", lg), ("dv", deb)]) if nb == 0 else [("ac", aprev)])
        aprev = ctx.step("scalar", lambda sc_, s=s: sc_.activation(
            sfb[0:64, s], ctx.pool["PC"][0:64, 0:512], AF.Sigmoid), waits=[("pe", p)])
        dbias = ctx.step("vector", lambda v, s=s: v.tensor_scalar(
            sfb[0:64, s], sfb[0:64, s], ebcol[0:64, 0:1], None, op0=ALU.add),
            waits=[("ac", aprev), ("cst", le2)] + ([("dv", dbias)] if dbias else []))

    if phases <= 2:
        return [("dv", dbias)]

    sf = ctx.carve("f32", 64)
    sco = ctx.carve("f32", 64)
    eqb = ctx.carve("f32", 64)
    sf2 = ctx.carve("f32", 64)
    tmpb = ctx.carve("f32", 64)
    rep = ctx.carve("f32", 64)
    sm = ctx.carve("f32", 64)
    sf3 = sf.rearrange("p (g e) -> p g e", g=8)
    dlast = None
    import os
    _rt_n = int(os.environ.get("L2FA_RT_TILES", NT))
    _rt_ops = int(os.environ.get("L2FA_RT_OPS", 99))
    for i in range(_rt_n):
        w0 = ([("dv", dbias), ("dv", deb)] if dlast is None else [("dv", dlast)])
        pt = ctx.step("tensor", lambda t, i=i: t.matmul(
            ctx.pool["PA"][:, 0:64], sfb[0:64, i * 128:(i + 1) * 128],
            C["ident"][0:64, 0:64], start=True, stop=True), waits=w0)
        d = ctx.step("vector", lambda v: v.tensor_copy(sf, ctx.pool["PA"][:, 0:64]),
                     waits=[("pe", pt)] + w0)
        d = ctx.step("vector", lambda v: v.tensor_tensor(sco, sf, ebt, ALU.subtract),
                     waits=[("dv", d)])
        d = ctx.step("vector", lambda v: v.tensor_reduce(
            sm[:, 0:8], sf3, axis=AX.X, op=ALU.max), waits=[("dv", d)])
        d = ctx.step("vector", lambda v: v.tensor_tensor(
            eqb.rearrange("p (g e) -> p g e", g=8), sf3,
            sm[:, 0:8].rearrange("p (g o) -> p g o", o=1).to_broadcast([128, 8, 8]),
            ALU.is_equal), waits=[("dv", d)])
        d = ctx.step("vector", lambda v: v.scalar_tensor_tensor(
            sf2, eqb, -1e30, sf, op0=ALU.mult, op1=ALU.add), waits=[("dv", d)])
        d = ctx.step("vector", lambda v: v.tensor_reduce(
            sm[:, 8:16], sf2.rearrange("p (g e) -> p g e", g=8), axis=AX.X, op=ALU.max),
            waits=[("dv", d)])
        d = ctx.step("vector", lambda v: v.tensor_tensor(
            sm[:, 16:24], sm[:, 0:8], sm[:, 8:16], ALU.add), waits=[("dv", d)])
        d = ctx.step("vector", lambda v: v.max(out=sm[:, 24:32], in_=sm[:, 16:24]),
                     waits=[("dv", d)])
        d = ctx.step("vector", lambda v: v.memset(sm[:, 28:32], 1e30), waits=[("dv", d)])
        d = ctx.step("vector", lambda v: v.match_replace(
            out=sm[:, 32:40], in_to_replace=sm[:, 24:32], in_values=sm[:, 16:24],
            imm_value=NEG), waits=[("dv", d)])
        d = ctx.step("vector", lambda v: v.tensor_scalar(
            sm[:, 32:40], sm[:, 32:40], -1e29, None, op0=ALU.is_lt), waits=[("dv", d)])
        d = ctx.step("vector", lambda v: v.tensor_tensor(
            tmpb.rearrange("p (g e) -> p g e", g=8), sf3,
            sm[:, 32:40].rearrange("p (g o) -> p g o", o=1).to_broadcast([128, 8, 8]),
            ALU.mult), waits=[("dv", d)])
        d = ctx.step("vector", lambda v: v.max(out=sm[:, 40:48], in_=tmpb), waits=[("dv", d)])
        d = ctx.step("vector", lambda v: v.match_replace(
            out=rep, in_to_replace=sm[:, 40:48], in_values=tmpb, imm_value=NEG),
            waits=[("dv", d)])
        d = ctx.step("vector", lambda v: v.tensor_scalar(
            rep, rep, -1e29, None, op0=ALU.is_lt), waits=[("dv", d)])
        d = ctx.step("vector", lambda v: v.tensor_tensor(tmpb, sco, rep, ALU.mult),
                     waits=[("dv", d)])
        d = ctx.step("vector", lambda v: v.tensor_reduce(
            sm[:, 48:49], tmpb, axis=AX.X, op=ALU.add), waits=[("dv", d)])
        d = ctx.step("vector", lambda v: v.tensor_scalar(
            sm[:, 48:49], sm[:, 48:49], 1e-20, None, op0=ALU.add), waits=[("dv", d)])
        d = ctx.step("vector", lambda v: v.reciprocal(sm[:, 49:50], sm[:, 48:49]),
                     waits=[("dv", d)])
        d = ctx.step("vector", lambda v: v.tensor_scalar(
            tmpb, tmpb, sm[:, 49:50], None, op0=ALU.mult), waits=[("dv", d)])
        d = ctx.step("vector", lambda v: v.tensor_scalar(
            tmpb, tmpb, 2.5, None, op0=ALU.mult), waits=[("dv", d)])
        if _rt_ops < 1:
            dlast = d
            continue
        pm = ctx.step("tensor", lambda t: t.transpose(
            ctx.pool["PB"][0:64, 0:128], rep, C["ident"][:, :]), waits=[("dv", d)])
        dm = ctx.step("vector", lambda v: v.tensor_copy(
            ctx.evbf[0][0:64, 0:128], ctx.pool["PB"][0:64, 0:128]),
            waits=[("pe", pm), ("sy0", ctx.cnt("sy0"))])
        if _rt_ops < 2:
            dlast = dm
            continue
        ctx.dma("sync", scr["mem"][:, i * 128:(i + 1) * 128], ctx.evbf[0][0:64, 0:128],
                waits=[("dv", dm)], sem_name="sy0")
        if _rt_ops < 3:
            dlast = dm
            continue
        pw = ctx.step("tensor", lambda t: t.transpose(
            ctx.pool["PB"][0:64, 128:256], tmpb, C["ident"][:, :]),
            waits=[("dv", dm)])
        dw = ctx.step("vector", lambda v: v.tensor_copy(
            scr6[0][0:64, 0:128], ctx.pool["PB"][0:64, 128:256]),
            waits=[("pe", pw), ("sy1", ctx.cnt("sy1")), ("dv", dm)])
        if _rt_ops < 4:
            dlast = dw
            continue
        ctx.dma("sync", scr["w"][:, i * 128:(i + 1) * 128], scr6[0][0:64, 0:128],
                waits=[("dv", dw)], sem_name="sy1")
        dlast = dw

    rt_done = [("dv", dlast), ("sy0", ctx.cnt("sy0")), ("sy1", ctx.cnt("sy1"))]
    if phases <= 3:
        return rt_done
    _, gW = load_weight_bf16(ctx, wd["sgT"], D, D, out=ctx.wtiles, waits=rt_done)
    ps2 = [ctx.pool["PC"], ctx.pool["PD"]]
    i2 = 0
    dd_prev = {}
    for m in range(8):
        for nb in range(8):
            s = slice(nb * 512, nb * 512 + 512)
            w = (rt_done + [("gs", gW)]) if i2 < 2 else [("ac", dd_prev[i2 - 2])]

            def mmg(t, m=m, s=s, i2=i2):
                for k in range(8):
                    h = t.matmul(ps2[i2 % 2][:, :], ctx.wtiles[k][:, m * 128:(m + 1) * 128],
                                 xln[k][:, s], start=(k == 0), stop=(k == 7))
                return h
            p = ctx.step("tensor", mmg, waits=w)
            a = ctx.step("scalar", lambda sc_, i2=i2: sc_.activation(
                ctx.evbf[i2 % 2][:, 0:512], ps2[i2 % 2][:, :], AF.Silu),
                waits=[("pe", p), (f"sy{i2 % 2}", ctx.cnt(f"sy{i2 % 2}"))])
            ctx.dma("sync", scr["gact"][m * 128:(m + 1) * 128, s],
                    ctx.evbf[i2 % 2][:, 0:512], waits=[("ac", a)], sem_name=f"sy{i2 % 2}")
            dd_prev[i2] = a
            i2 += 1
    gact_done = [("ac", dd_prev[63]), ("sy0", ctx.cnt("sy0")), ("sy1", ctx.cnt("sy1"))]
    if phases <= 4:
        return gact_done

    _, gW2 = load_weight_bf16(ctx, wd["suT"], D, D, out=ctx.wtiles, waits=gact_done)
    gb = [ctx.carve("bf2", 512) for _ in range(2)]
    i2 = 0
    dd_prev2 = {}
    for m in range(8):
        for nb in range(8):
            s = slice(nb * 512, nb * 512 + 512)
            w = (gact_done + [("gs", gW2)]) if i2 < 2 else [("dv", dd_prev2[i2 - 2])]
            lga = ctx.dma("sync", gb[i2 % 2], scr["gact"][m * 128:(m + 1) * 128, s],
                          waits=w, sem_name=f"kt{i2 % 2}")

            def mmu(t, m=m, s=s, i2=i2):
                for k in range(8):
                    h = t.matmul(ps2[i2 % 2][:, :], ctx.wtiles[k][:, m * 128:(m + 1) * 128],
                                 xln[k][:, s], start=(k == 0), stop=(k == 7))
                return h
            p = ctx.step("tensor", mmu, waits=w)
            dmul = ctx.step("vector", lambda v, i2=i2: v.tensor_tensor(
                ctx.evbf[i2 % 2][:, 0:512], ps2[i2 % 2][:, :], gb[i2 % 2], ALU.mult),
                waits=[("pe", p), (f"kt{i2 % 2}", lga), (f"sy{i2 % 2}", ctx.cnt(f"sy{i2 % 2}"))])
            ctx.dma("sync", scr["gp"][m * 128:(m + 1) * 128, s], ctx.evbf[i2 % 2][:, 0:512],
                    waits=[("dv", dmul)], sem_name=f"sy{i2 % 2}")
            dd_prev2[i2] = dmul
            i2 += 1
    up_done = [("dv", dd_prev2[63]), ("sy0", ctx.cnt("sy0")), ("sy1", ctx.cnt("sy1"))]
    if phases <= 5:
        return up_done

    lgp = None
    for k in range(8):
        lgp = ctx.dma("sync", xln[k][:, :], scr["gp"][k * 128:(k + 1) * 128, :],
                      waits=(up_done if k == 0 else ()), sem_name="qg")
    return emit_out_proj(ctx, xln, wd["sdT"], xT_out, [("qg", lgp)], res_dram=xT_in)


def decl_l2f_a(nc):
    wd = {
        "gwt": nc.dram_tensor("l2fa_gwt", [128, 512], F32, kind="ExternalInput").ap(),
        "ebias": nc.dram_tensor("l2fa_eb", [64], F32, kind="ExternalInput").ap(),
        "sgT": nc.dram_tensor("l2fa_sgT", [D, D], F32, kind="ExternalInput").ap(),
        "suT": nc.dram_tensor("l2fa_suT", [D, D], F32, kind="ExternalInput").ap(),
        "sdT": nc.dram_tensor("l2fa_sdT", [D, D], F32, kind="ExternalInput").ap(),
    }
    scr = {
        "xlnr": nc.dram_tensor("l2fa_xlnr", [T, D], BF16, kind="ExternalOutput").ap(),
        "mem": nc.dram_tensor("l2fa_mem", [64, T], BF16, kind="ExternalOutput").ap(),
        "w": nc.dram_tensor("l2fa_w", [64, T], F32, kind="ExternalOutput").ap(),
        "gact": nc.dram_tensor("l2fa_gact", [D, T], BF16, kind="Internal").ap(),
        "gp": nc.dram_tensor("l2fa_gp", [D, T], BF16, kind="Internal").ap(),
    }
    return wd, scr


def stage_l2f_a(inp):
    f = np.ascontiguousarray
    gw = inp["L2_ffn_gate_weight"].astype(np.float32)
    gwt = gw.T.reshape(8, 128, 64).transpose(1, 0, 2).reshape(128, 512)
    return {
        "l2fa_gwt": f(gwt),
        "l2fa_eb": f(inp["L2_ffn_e_bias"].astype(np.float32)),
        "l2fa_sgT": f(inp["L2_ffn_shared_gate"].astype(np.float32).T),
        "l2fa_suT": f(inp["L2_ffn_shared_up"].astype(np.float32).T),
        "l2fa_sdT": f(inp["L2_ffn_shared_down"].astype(np.float32).T),
    }


EMITTERS["l2f_a"] = emit_l2f_a
DECLS["l2f_a"] = decl_l2f_a
STAGERS["l2f_a"] = stage_l2f_a


def emit_l2f_b(ctx, xg, memg, wg, w1t, w3t, w2t, acc, queue, C):
    g = "l2fb"
    scr6 = ctx.scr6
    xln = ctx.xln
    GT = 8 * T
    NTILE = GT // 128

    z0 = ctx.carve("f32", 1024)
    dz = ctx.step("vector", lambda v: v.memset(z0, 0.0), waits=C["ready"])
    for i in range(NTILE):
        ctx.dma("sync", acc[i * 128:(i + 1) * 128, :], z0,
                waits=([("dv", dz)] if i == 0 else ()), sem_name="az")
    ctx.dma("sync", acc[GT:GT + 1, :], z0[0:1, :], sem_name="az")
    qsb = ctx.sb(g + "qs", [8, 1280], U32)
    dq = ctx.step("vector", lambda v: v.memset(qsb[:, :], float(GT)), waits=C["ready"])
    az_total = ctx.dma("sync", queue.rearrange("(a b) -> a b", b=1280), qsb[:, :],
                       waits=[("dv", dq)], sem_name="az")

    CH = 2048
    basec = scr6[4][:, 0:1]
    dz2 = ctx.step("vector", lambda v: v.memset(basec, 0.0), waits=C["ready"])
    zb = ctx.step("vector", lambda v: v.memset(xln[1][0:8, 0:CH], 0.0), waits=C["ready"])
    pcol = scr6[4][:, 1:2]
    gp0 = ctx.step("gpsimd", lambda gp: gp.iota(pcol, pattern=[[1, 1]], base=0,
                   channel_multiplier=1, allow_small_or_imprecise_dtypes=True),
                   waits=C["ready"], inc_sem="gsi", inc=1)
    qoff = scr6[4][:, 2:10]
    gp1 = ctx.step("gpsimd", lambda gp: gp.iota(qoff, pattern=[[1280, 8]], base=0,
                   channel_multiplier=0, allow_small_or_imprecise_dtypes=True),
                   waits=C["ready"], inc_sem="gsi", inc=1)
    slotu = ctx.sb(g + "slotu", [128, 8], U32)
    tidu = ctx.sb(g + "tidu", [128, 1], U32)
    csb = ctx._ar["f32b"]
    dlast = None
    qs = None
    for ch in range(GT // CH):
        w = [("gsi", gp0), ("dv", dz2), ("dv", zb)] + ([("dv", dlast)] if dlast else [])
        lm = ctx.dma("sync", xln[0][0:8, 0:CH], memg[:, ch * CH:(ch + 1) * CH],
                     waits=w, sem_name="kt0")
        d = ctx.step("vector", lambda v: v.tensor_copy(csb[0:8, CH:2 * CH], xln[0][0:8, 0:CH]),
                     waits=[("kt0", lm)] + w)
        d = ctx.step("vector", lambda v: v.tensor_tensor_scan(
            csb[0:8, 0:CH], csb[0:8, CH:2 * CH], xln[1][0:8, 0:CH], 0.0,
            op0=ALU.add, op1=ALU.add), waits=[("dv", d)])
        d = ctx.step("vector", lambda v: v.tensor_scalar(
            csb[0:8, 0:CH], csb[0:8, 0:CH], basec[0:8, 0:1], None, op0=ALU.add),
            waits=[("dv", d)])
        d = ctx.step("vector", lambda v: v.tensor_copy(
            basec[0:8, 0:1], csb[0:8, CH - 1:CH]), waits=[("dv", d)])
        dme = d
        for k in range(CH // 128):
            gtile = ch * (CH // 128) + k
            DUMP = float(CAP + gtile)
            pm_ = ctx.step("tensor", lambda t, k=k: t.matmul(
                ctx.pool["PC"][:, 0:8], csb[0:8, CH + k * 128:CH + (k + 1) * 128],
                C["ident"][0:8, 0:8], start=True, stop=True), waits=[("dv", dme)])
            pc_ = ctx.step("tensor", lambda t, k=k: t.matmul(
                ctx.pool["PD"][:, 0:8], csb[0:8, k * 128:(k + 1) * 128],
                C["ident"][0:8, 0:8], start=True, stop=True), waits=[("pe", pm_)])
            dme = ctx.step("vector", lambda v: v.tensor_copy(
                scr6[0][:, 0:8], ctx.pool["PC"][:, 0:8]), waits=[("pe", pm_)])
            dme = ctx.step("vector", lambda v: v.tensor_copy(
                scr6[0][:, 8:16], ctx.pool["PD"][:, 0:8]), waits=[("pe", pc_), ("dv", dme)])
            dme = ctx.step("vector", lambda v: v.tensor_scalar(
                scr6[0][:, 16:24], scr6[0][:, 8:16], 1.0, None, op0=ALU.subtract),
                waits=[("dv", dme)])
            dme = ctx.step("vector", lambda v: v.tensor_scalar(
                scr6[0][:, 24:32], scr6[0][:, 16:24], float(CAP - 1), None, op0=ALU.is_le),
                waits=[("dv", dme)])
            dme = ctx.step("vector", lambda v: v.tensor_tensor(
                scr6[0][:, 24:32], scr6[0][:, 24:32], scr6[0][:, 0:8], ALU.mult),
                waits=[("dv", dme)])
            dme = ctx.step("vector", lambda v, DUMP=DUMP: v.tensor_scalar(
                scr6[0][:, 16:24], scr6[0][:, 16:24], DUMP, None, op0=ALU.subtract),
                waits=[("dv", dme)])
            dme = ctx.step("vector", lambda v: v.tensor_tensor(
                scr6[0][:, 16:24], scr6[0][:, 16:24], scr6[0][:, 24:32], ALU.mult),
                waits=[("dv", dme)])
            dme = ctx.step("vector", lambda v, DUMP=DUMP: v.tensor_scalar(
                scr6[0][:, 16:24], scr6[0][:, 16:24], DUMP, None, op0=ALU.add),
                waits=[("dv", dme)])
            dme = ctx.step("vector", lambda v: v.tensor_tensor(
                scr6[0][:, 16:24], scr6[0][:, 16:24], qoff, ALU.add),
                waits=[("dv", dme), ("gsi", gp1)])
            dme = ctx.step("vector", lambda v: v.tensor_copy(slotu[:, :], scr6[0][:, 16:24]),
                          waits=[("dv", dme), ("qs", ctx.cnt("qs"))])
            dme = ctx.step("vector", lambda v, gtile=gtile: v.tensor_scalar(
                tidu[:, :], pcol, float(gtile * 128), None, op0=ALU.add), waits=[("dv", dme)])
            for j in range(8):
                qs = ctx.dma("gpsimd", queue.rearrange("(a o) -> a o", o=1), tidu[:, :],
                             waits=([("dv", dme), ("az", az_total)] if j == 0 else ()),
                             sem_name="qs", out_offset=IOX(slotu[:, j:j + 1], 0))
        dlast = dme
    qs_total = qs

    qcol = ctx.sb(g + "qcol", [128, 8], U32)
    wqu = ctx.sb(g + "wqu", [128, 8], U32)
    xe_tm = ctx._ar["bf"][:, 0:8192]
    xe_fm = ctx._ar["bf"][:, 8192:16384]
    htm = ctx._ar["bf2"]
    yef = [ctx.carve("f32", 1024), ctx.carve("f32", 1024)]
    psT = [ctx.pool["PC"], ctx.pool["PD"]]
    dexp = None
    for j in range(8):
        wv1 = [ctx.wtiles[r][:, 0:512] for r in range(8)]
        _, gw1 = load_weight_bf16(ctx, w1t[j], D, 512, out=wv1,
                                  waits=([("qs", qs_total)] if dexp is None
                                         else [("dv", dexp), ("sa", ctx.cnt("sa"))]))
        wv3 = [ctx.wtiles[r][:, 512:1024] for r in range(8)]
        _, gw3 = load_weight_bf16(ctx, w3t[j], D, 512, out=wv3)
        lw2 = None
        for q in range(4):
            lw2 = ctx.dma("gpsimd", xln[1][:, q * 1024:(q + 1) * 1024],
                          w2t[j][q * 128:(q + 1) * 128, :],
                          waits=(([("qs", qs_total)] if dexp is None else [("dv", dexp)])
                                 if q == 0 else ()), sem_name="gs")
        lq = None
        for k in range(8):
            lq = ctx.dma("sync", qcol[:, k:k + 1],
                         queue[j * 1280 + k * 128:j * 1280 + (k + 1) * 128]
                         .rearrange("(p o) -> p o", o=1),
                         waits=(([("qs", qs_total), ("sa", ctx.cnt("sa"))] +
                                 ([("dv", dexp)] if dexp else [])) if k == 0 else ()),
                         sem_name="kt1")
        dwq = ctx.step("vector", lambda v, j=j: v.tensor_scalar(
            scr6[3][:, 0:8], qcol[:, :], float(j * (GT + 1)), None, op0=ALU.add),
            waits=[("kt1", lq)])
        dwq = ctx.step("vector", lambda v: v.tensor_copy(wqu[:, :], scr6[3][:, 0:8]),
                       waits=[("dv", dwq)])
        lwq = None
        for k in range(8):
            lwq = ctx.dma("gpsimd", scr6[1][:, k:k + 1], wg.rearrange("(a o) -> a o", o=1),
                          waits=([("dv", dwq)] if k == 0 else ()), sem_name="gla",
                          in_offset=IOX(wqu[:, k:k + 1], 0))
        xeg = None
        for k in range(8):
            xeg = ctx.dma("gpsimd", xe_tm[:, k * 1024:(k + 1) * 1024], xg,
                          waits=([("kt1", lq)] if k == 0 else ()), sem_name="gla",
                          in_offset=IOX(qcol[:, k:k + 1], 0))
        dtr = {}
        for i_ in range(64):
            k, c = i_ // 8, i_ % 8
            pt_ = ctx.step("tensor", lambda t, k=k, c=c, i_=i_: t.matmul(
                psT[i_ % 2][:, 0:128],
                xe_tm[:, k * 1024 + c * 128:k * 1024 + (c + 1) * 128], C["identb"][:, :],
                start=True, stop=True),
                waits=([("gla", xeg)] if i_ < 2 else [("dv", dtr[i_ - 2])]))
            dtr[i_] = ctx.step("vector", lambda v, k=k, c=c, i_=i_: v.tensor_copy(
                xe_fm[:, c * 1024 + k * 128:c * 1024 + (k + 1) * 128], psT[i_ % 2][:, 0:128]),
                waits=[("pe", pt_)])
        dh = {}
        for k in range(8):
            def mm1(t, k=k):
                for c in range(8):
                    h = t.matmul(ctx.pool["PA"][:, 0:512],
                                 xe_fm[:, c * 1024 + k * 128:c * 1024 + (k + 1) * 128],
                                 ctx.wtiles[c][:, 0:512], start=(c == 0), stop=(c == 7))
                return h
            p1_ = ctx.step("tensor", mm1,
                           waits=([("dv", dtr[63]), ("gs", lw2)] if k == 0 else [("dv", dh[k - 1])]))
            a1 = ctx.step("scalar", lambda sc_: sc_.activation(
                scr6[2][:, 0:512], ctx.pool["PA"][:, 0:512], AF.Silu), waits=[("pe", p1_)])

            def mm3(t, k=k):
                for c in range(8):
                    h = t.matmul(ctx.pool["PA"][:, 512:1024],
                                 xe_fm[:, c * 1024 + k * 128:c * 1024 + (k + 1) * 128],
                                 ctx.wtiles[c][:, 512:1024], start=(c == 0), stop=(c == 7))
                return h
            p3_ = ctx.step("tensor", mm3, waits=[("pe", p1_)])
            dh[k] = ctx.step("vector", lambda v, k=k: v.tensor_tensor(
                htm[:, k * 512:(k + 1) * 512], scr6[2][:, 0:512],
                ctx.pool["PA"][:, 512:1024], ALU.mult), waits=[("ac", a1), ("pe", p3_)])
        dt2 = {}
        for i_ in range(32):
            k, q = i_ // 4, i_ % 4
            pt_ = ctx.step("tensor", lambda t, k=k, q=q, i_=i_: t.matmul(
                psT[i_ % 2][:, 0:128],
                htm[:, k * 512 + q * 128:k * 512 + (q + 1) * 128], C["identb"][:, :],
                start=True, stop=True),
                waits=([("dv", dh[7])] if i_ < 2 else [("dv", dt2[i_ - 2])]))
            dt2[i_] = ctx.step("vector", lambda v, k=k, q=q, i_=i_: v.tensor_copy(
                xln[2][:, q * 1024 + k * 128:q * 1024 + (k + 1) * 128], psT[i_ % 2][:, 0:128]),
                waits=[("pe", pt_)])
        dy = dt2[31]
        for k in range(8):
            def mmy(t, k=k):
                for half in range(2):
                    for q in range(4):
                        h = t.matmul(ctx.pool["PB"][:, half * 512:(half + 1) * 512],
                                     xln[2][:, q * 1024 + k * 128:q * 1024 + (k + 1) * 128],
                                     xln[1][:, q * 1024 + half * 512:q * 1024 + (half + 1) * 512],
                                     start=(q == 0), stop=(q == 3))
                return h
            py = ctx.step("tensor", mmy, waits=[("dv", dy), ("gs", lw2)])
            dy = ctx.step("vector", lambda v, k=k: v.tensor_scalar(
                yef[k % 2][:, 0:512], ctx.pool["PB"][:, 0:512], scr6[1][:, k:k + 1],
                None, op0=ALU.mult),
                waits=[("pe", py), ("gla", xeg), ("sa", ctx.cnt("sa"))])
            dy = ctx.step("vector", lambda v, k=k: v.tensor_scalar(
                yef[k % 2][:, 512:1024], ctx.pool["PB"][:, 512:1024], scr6[1][:, k:k + 1],
                None, op0=ALU.mult), waits=[("dv", dy)])
            ctx.dma("gpsimd", acc, yef[k % 2],
                    waits=[("dv", dy), ("az", ctx.cnt("az")), ("sa", ctx.cnt("sa"))],
                    sem_name="sa", out_offset=IOX(qcol[:, k:k + 1], 0), compute_op=ALU.add)
        dexp = dy


def build_l2fb_program():
    nc = bass.Bass("TRN2", target_bir_lowering=False)
    GT = 8 * T
    xg = nc.dram_tensor("xg", [GT + 1, D], BF16, kind="ExternalInput").ap()
    memg = nc.dram_tensor("memg", [8, GT], BF16, kind="ExternalInput").ap()
    wg = nc.dram_tensor("wg", [8 * (GT + 1)], F32, kind="ExternalInput").ap()
    w1t = nc.dram_tensor("w1t", [8, D, 512], F32, kind="ExternalInput").ap()
    w3t = nc.dram_tensor("w3t", [8, D, 512], F32, kind="ExternalInput").ap()
    w2t = nc.dram_tensor("w2t", [8, 512, D], F32, kind="ExternalInput").ap()
    acc = nc.dram_tensor("acc", [GT + 1, D], F32, kind="ExternalOutput").ap()
    queue = nc.dram_tensor("l2fb_q", [8 * 1280], U32, kind="Internal").ap()
    cd = decl_consts(nc)
    es = ExitStack()
    with es:
        block = es.enter_context(nc.Block())
        ctx = Ctx(nc, block, es)
        ctx.make_pool()
        C = emit_consts(ctx, cd)
        ctx.gate(C["ready"])
        emit_l2f_b(ctx, xg, memg, wg, w1t, w3t, w2t, acc, queue, C)
        snap = dict(ctx._cnt)

        def fin(e):
            for name, v in snap.items():
                if v > 0:
                    e.wait_ge(ctx.sem(name), v)
        block.sync(fin)
    return nc


def emit_l2f_c(ctx, xT_in, xT_out, wd, C, scr):
    acc8 = wd["accs"]
    ab = [ctx.carve("f32", 1024), ctx.carve("f32", 1024)]
    sumb = ctx.carve("f32b", 1024)
    dprev = None
    for i in range(NT):
        dadd = {}
        for j in range(8):
            w = []
            if j < 2:
                if dprev is not None:
                    w.append(("dv", dprev))
            else:
                w.append(("dv", dadd[j - 2]))
            lj = ctx.dma("sync", ab[j % 2], acc8[j, i * 128:(i + 1) * 128, :],
                         waits=w, sem_name=f"kt{j % 2}")
            if j == 0:
                dadd[0] = ctx.step("vector", lambda v: v.tensor_copy(sumb, ab[0]),
                                   waits=[("kt0", lj)] + w)
            else:
                dadd[j] = ctx.step("vector", lambda v, j=j: v.tensor_tensor(
                    sumb, sumb, ab[j % 2], ALU.add),
                    waits=[(f"kt{j % 2}", lj), ("dv", dadd[j - 1])])
        dres = {}
        for c in range(8):
            w = [("dv", dadd[7])] if c < 2 else [("dv", dres[c - 2])]
            pt_ = ctx.step("tensor", lambda t, c=c: t.transpose(
                (ctx.pool["PC"] if c % 2 == 0 else ctx.pool["PD"])[:, 0:128],
                sumb[:, c * 128:(c + 1) * 128], C["ident"][:, :]), waits=w)
            lr = ctx.dma("sync", ctx.scr6[2 + c % 2][:, 0:128],
                         xT_in[c * 128:(c + 1) * 128, i * 128:(i + 1) * 128],
                         waits=w, sem_name=f"xr{c % 2}")
            dres[c] = ctx.step("vector", lambda v, c=c: v.tensor_tensor(
                ctx.scr6[c % 2][:, 0:128],
                (ctx.pool["PC"] if c % 2 == 0 else ctx.pool["PD"])[:, 0:128],
                ctx.scr6[2 + c % 2][:, 0:128], ALU.add),
                waits=[("pe", pt_), (f"xr{c % 2}", lr),
                       (f"sy{c % 2}", ctx.cnt(f"sy{c % 2}"))])
            ctx.dma("sync", xT_out[c * 128:(c + 1) * 128, i * 128:(i + 1) * 128],
                    ctx.scr6[c % 2][:, 0:128], waits=[("dv", dres[c])], sem_name=f"sy{c % 2}")
        dprev = dres[7]
    return [("dv", dprev), ("sy0", ctx.cnt("sy0")), ("sy1", ctx.cnt("sy1"))]


def decl_l2f_c(nc):
    wd = {"accs": nc.dram_tensor("l2fc_accs", [8, T, D], F32, kind="ExternalInput").ap()}
    return wd, {}


def stage_l2f_c(inp):
    return {}


EMITTERS["l2f_c"] = emit_l2f_c
DECLS["l2f_c"] = decl_l2f_c
STAGERS["l2f_c"] = stage_l2f_c



L2F_L1A = ["l0_attn", "l0_ffn", "l1_attn"]
L2F_L1B = ["l1_ffn", "l2_attn", "l2f_a"]
L2F_L3 = ["l2f_c", "l3_attn", "l3_ffn"]


def kernel(**inp):
    import ml_dtypes
    x = inp["L0_attn_x"]
    n = 8
    xT = [np.ascontiguousarray(x[c].T.astype(np.float32)) for c in range(n)]
    consts = host_consts()

    nc = build_program(L2F_L1A)
    stage = dict(consts)
    for b in set(L2F_L1A):
        stage.update(STAGERS[b](inp))
    in_maps = [dict(stage, xT_in=xT[c]) for c in range(n)]
    res = run_bass_kernel_spmd(nc, in_maps, core_ids=list(range(n)))
    xT = [np.asarray(res.results[c]["xT_out"]) for c in range(n)]

    nc = build_program(L2F_L1B)
    stage = dict(consts)
    for b in set(L2F_L1B):
        stage.update(STAGERS[b](inp))
    in_maps = [dict(stage, xT_in=xT[c]) for c in range(n)]
    res = run_bass_kernel_spmd(nc, in_maps, core_ids=list(range(n)))
    xpT = [np.asarray(res.results[c]["xT_out"]) for c in range(n)]
    xlnr = [np.asarray(res.results[c]["l2fa_xlnr"]) for c in range(n)]
    mem = [np.asarray(res.results[c]["l2fa_mem"]) for c in range(n)]
    wde = [np.asarray(res.results[c]["l2fa_w"]) for c in range(n)]

    xgz = np.ascontiguousarray(np.concatenate(
        xlnr + [np.zeros((1, D), ml_dtypes.bfloat16)], axis=0))
    w1 = inp["L2_ffn_w1"].astype(np.float32)
    w3 = inp["L2_ffn_w3"].astype(np.float32)
    w2 = inp["L2_ffn_w2"].astype(np.float32)
    nc2 = build_l2fb_program()
    in2 = []
    for c in range(n):
        memg = np.concatenate([mem[cp][8 * c:8 * c + 8] for cp in range(n)], axis=1)
        wgc = np.concatenate([wde[cp][8 * c:8 * c + 8] for cp in range(n)] +
                             [np.zeros((8, 1), np.float32)], axis=1).reshape(-1)
        in2.append(dict(consts,
                        xg=xgz, memg=np.ascontiguousarray(memg),
                        wg=np.ascontiguousarray(wgc),
                        w1t=np.ascontiguousarray(w1[8 * c:8 * c + 8].transpose(0, 2, 1)),
                        w3t=np.ascontiguousarray(w3[8 * c:8 * c + 8].transpose(0, 2, 1)),
                        w2t=np.ascontiguousarray(w2[8 * c:8 * c + 8].transpose(0, 2, 1))))
    res2 = run_bass_kernel_spmd(nc2, in2, core_ids=list(range(n)))
    accs = [np.asarray(res2.results[c]["acc"]) for c in range(n)]

    nc3 = build_program(L2F_L3)
    stage3 = dict(consts)
    for b in set(L2F_L3):
        stage3.update(STAGERS[b](inp))
    in3 = []
    for c in range(n):
        a8 = np.stack([accs[cp][c * T:(c + 1) * T] for cp in range(n)], axis=0)
        in3.append(dict(stage3, xT_in=xpT[c], l2fc_accs=np.ascontiguousarray(a8)))
    res3 = run_bass_kernel_spmd(nc3, in3, core_ids=list(range(n)))
    out = np.stack([np.asarray(res3.results[c]["xT_out"]).T for c in range(n)], axis=0)
    return np.ascontiguousarray(out.astype(np.float32))
